# version 21
# baseline (speedup 1.0000x reference)
import numpy as np
from contextlib import ExitStack

import concourse.bass as bass
import concourse.mybir as mybir
from concourse.bass_utils import run_bass_kernel_spmd

F32 = mybir.dt.float32
BF16 = mybir.dt.bfloat16
I8 = mybir.dt.int8
AF = mybir.ActivationFunctionType
ALU = mybir.AluOpType
AX = mybir.AxisListType

S_TOK = 4096
D = 2048
H = 16
INTER = 8192
NT = S_TOK // 128
NCH = D // 128
NI = INTER // 128
NTC = 8
EPS = 1e-5
QSCALE = float(1.0 / np.sqrt(128.0))
GRP = 4

ENGS = ("tensor", "scalar", "vector", "sync", "gpsimd")
SEM_NAMES = ("d_xa", "d_xb", "d_sa", "d_sb", "d_sv", "d_sg", "d_wa", "d_wb",
             "d_wv", "d_tr", "c_t", "c_s", "c_v", "c_g")


class Prog:
    def __init__(self, nc):
        self.nc = nc
        self.ops = {e: [] for e in ENGS}
        self.cnt = {s: 0 for s in SEM_NAMES}
        self.stack = ExitStack()
        self.S = {s: self.stack.enter_context(nc.semaphore(s)) for s in SEM_NAMES}

    def _emit(self, eng, fn, waits, incs):
        S = self.S
        waits = tuple((s, t) for s, t in waits if t > 0)
        incs = tuple(incs)

        def run(e):
            for s, t in waits:
                e.wait_ge(S[s], t)
            ins = fn(e)
            for s, n in incs:
                ins.then_inc(S[s], n)
            return ins

        self.ops[eng].append(run)

    def op(self, eng, fn, wait=(), inc=(), chain=True):
        waits = list(wait)
        incs = list(inc)
        if eng == "scalar":
            assert not incs
            if chain:
                waits.append(("c_s", self.cnt["c_s"]))
            self.cnt["c_s"] += 1
            incs.append(("c_s", 1))
        elif eng == "vector":
            assert not incs
            if chain:
                waits.append(("c_v", self.cnt["c_v"]))
            self.cnt["c_v"] += 1
            incs.append(("c_v", 1))
        else:
            for s, n in incs:
                self.cnt[s] += n
        self._emit(eng, fn, waits, incs)

    def dma(self, eng, fn, sem, wait=(), batch=False):
        waits = list(wait)
        if not batch:
            waits.append((sem, self.cnt[sem]))
        self.cnt[sem] += 16
        self._emit(eng, fn, waits, [(sem, 16)])

    def wait(self, eng, wait):
        S = self.S
        waits = tuple((s, t) for s, t in wait if t > 0)
        if not waits:
            return

        def run(e):
            for s, t in waits:
                e.wait_ge(S[s], t)

        self.ops[eng].append(run)

    def val(self, s):
        return self.cnt[s]

    def phase_barrier(self):
        w = [(s, self.cnt[s]) for s in SEM_NAMES]
        for e in ENGS:
            self.wait(e, w)

    def flush_block(self):
        ops = self.ops
        with self.nc.Block() as b:
            @b.tensor
            def _(t):
                for f in ops["tensor"]:
                    f(t)

            @b.scalar
            def _(s):
                for f in ops["scalar"]:
                    f(s)

            @b.vector
            def _(v):
                for f in ops["vector"]:
                    f(v)

            @b.sync
            def _(sy):
                for f in ops["sync"]:
                    f(sy)

            @b.gpsimd
            def _(g):
                for f in ops["gpsimd"]:
                    f(g)
        self.ops = {e: [] for e in ENGS}



def _batched_ln(P, nb, tiles_ap, bnbuf, mvb, sqb, rstdb, nmbb, eps_t):
    for j in range(nb):
        for q in range(4):
            P.op("vector", lambda v, j=j, q=q, t=tiles_ap: v.bn_stats(
                out=bnbuf[:, j, q, :], in_=t(j)[:, q * 512:(q + 1) * 512]),
                chain=(j == 0 and q == 0))
    for j in range(nb):
        P.op("vector", lambda v, j=j: v.bn_aggr(
            out=mvb[:, j, :], in_=bnbuf[:, j, :, :]), chain=(j == 0))
    P.op("scalar", lambda s: s.activation(
        out=sqb[:], in_=mvb[:, :, 1], func=AF.Sqrt, bias=eps_t[:], scale=1.0),
        wait=[("c_v", P.val("c_v"))])
    P.op("vector", lambda v: v.reciprocal(out=rstdb[:], in_=sqb[:]),
         wait=[("c_s", P.val("c_s"))])
    P.op("vector", lambda v: v.scalar_tensor_tensor(
        out=nmbb[:], in0=mvb[:, :, 0], scalar=-1.0, in1=rstdb[:],
        op0=ALU.mult, op1=ALU.mult))


def _ln_stats(P, tile_ap, bnbuf, mv):
    for j in range(4):
        P.op("vector", lambda v, t=tile_ap, b=bnbuf, j=j: v.bn_stats(
            out=b[:, j, :], in_=t[:, j * 512:(j + 1) * 512]))
    P.op("vector", lambda v, b=bnbuf, m=mv: v.bn_aggr(out=m[:], in_=b[:]))


def _ln_scalebias(P, mv, sq, rstd, nmb, eps_t):
    P.op("scalar", lambda s, m=mv, q=sq, e=eps_t: s.activation(
        out=q[:], in_=m[:, 1:2], func=AF.Sqrt, bias=e[:], scale=1.0),
        wait=[("c_v", P.val("c_v"))])
    P.op("vector", lambda v, q=sq, r=rstd: v.reciprocal(out=r[:], in_=q[:]),
         wait=[("c_s", P.val("c_s"))])
    P.op("vector", lambda v, m=mv, r=rstd, o=nmb: v.tensor_scalar(
        out=o[:], in0=m[:, 0:1], scalar1=r[:], scalar2=-1.0,
        op0=ALU.mult, op1=ALU.mult))


class Banks:

    def __init__(self):
        self.free = {b: ("c_s", 0) for b in range(8)}

    def wait_for(self, banks):
        return [self.free[b] for b in banks]

    def set(self, banks, sem, tgt):
        for b in banks:
            self.free[b] = (sem, tgt)


def build(debug=0, max_phase=5):
    nc = bass.Bass("TRN2", target_bir_lowering=False)

    x_d = nc.dram_tensor("x", [S_TOK, D], F32, kind="ExternalInput")
    wqkv_d = nc.dram_tensor("wqkv", [D, 3 * D], F32, kind="ExternalInput")
    wo_d = nc.dram_tensor("wo", [D, D], F32, kind="ExternalInput")
    wg_d = nc.dram_tensor("wg", [D, INTER], I8, kind="ExternalInput")
    wu_d = nc.dram_tensor("wu", [D, INTER], I8, kind="ExternalInput")
    wd_d = nc.dram_tensor("wd", [INTER, D], I8, kind="ExternalInput")
    sg_d = nc.dram_tensor("sg", [128, NI], F32, kind="ExternalInput")
    su_d = nc.dram_tensor("su", [128, NI], F32, kind="ExternalInput")
    sd_d = nc.dram_tensor("sd", [128, NCH], F32, kind="ExternalInput")
    out_d = nc.dram_tensor("out", [S_TOK, D], F32, kind="ExternalOutput")

    sk = "ExternalOutput" if debug else "Internal"
    h1_d = nc.dram_tensor("h1d", [S_TOK, D], BF16, kind=sk)
    val_d = nc.dram_tensor("vald", [D, S_TOK], BF16, kind=sk)
    attn_d = nc.dram_tensor("attnd", [D, S_TOK], BF16, kind=sk)
    r1_d = nc.dram_tensor("r1d", [S_TOK, D], F32, kind=sk)
    h2_d = nc.dram_tensor("h2d", [S_TOK, D], BF16, kind=sk)
    down_d = nc.dram_tensor("downd", [D, S_TOK], BF16, kind=sk)
    qkv_d = nc.dram_tensor("qkvd", [3 * D, S_TOK], BF16, kind=sk)

    es = ExitStack()
    P = Prog(nc)
    es.enter_context(P.stack)

    sgt = es.enter_context(nc.sbuf_tensor("sgt", [128, NI], F32))
    sut = es.enter_context(nc.sbuf_tensor("sut", [128, NI], F32))
    sdt = es.enter_context(nc.sbuf_tensor("sdt", [128, NCH], F32))
    ones_bf = es.enter_context(nc.sbuf_tensor("ones_bf", [128, 128], BF16))
    eps_t = es.enter_context(nc.sbuf_tensor("eps_t", [128, 1], F32))
    pall = es.enter_context(nc.psum_tensor("pall", [128, 8, 512], F32))

    P.dma("gpsimd", lambda g: g.dma_start(
        out=sgt[:], in_=sg_d[:]), "d_wa")
    P.dma("gpsimd", lambda g: g.dma_start(
        out=sut[:], in_=su_d[:]), "d_wa")
    P.dma("gpsimd", lambda g: g.dma_start(
        out=sdt[:], in_=sd_d[:]), "d_wa")
    P.op("gpsimd", lambda g: g.memset(ones_bf[:], 1.0), inc=[("c_g", 1)])
    P.op("gpsimd", lambda g: g.memset(eps_t[:], EPS), inc=[("c_g", 1)])
    for _e in ("scalar", "vector", "tensor"):
        P.wait(_e, [("c_g", 2)])

    h1T_scope = ExitStack()
    h1T = h1T_scope.enter_context(nc.sbuf_tensor("h1T", [128, NCH, S_TOK], BF16))

    NB1 = 4
    with nc.sbuf_tensor("xt", [128, NB1, D], F32) as xt, \
         nc.sbuf_tensor("h1sb", [128, NB1, D], BF16) as h1sb, \
         nc.sbuf_tensor("bnbuf", [128, NB1, 4, 6], F32) as bnbuf, \
         nc.sbuf_tensor("mv1", [128, NB1, 2], F32) as mv1, \
         nc.sbuf_tensor("sq1", [128, NB1], F32) as sq1, \
         nc.sbuf_tensor("rstd1", [128, NB1], F32) as rstd1, \
         nc.sbuf_tensor("nmb1", [128, NB1], F32) as nmb1:

        nbat = NT // NB1
        s_done = {}
        v_done = {}
        st_done = {}
        for bt in range(nbat):
            xsem = "d_xa" if bt % 2 == 0 else "d_xb"
            ssem = "d_sa" if bt % 2 == 0 else "d_sb"
            osem = "d_sb" if bt % 2 == 0 else "d_sa"
            for j in range(NB1):
                tt = bt * NB1 + j
                P.dma("sync", lambda sy, tt=tt, j=j: sy.dma_start(
                    out=xt[:, j, :], in_=x_d[tt * 128:(tt + 1) * 128, :]),
                    xsem, batch=(j > 0),
                    wait=[("c_s", s_done.get(bt - 1, 0))] if j == 0 else [])
            x_tgt = P.val(xsem)
            P.wait("vector", [(xsem, x_tgt), ("c_g", 2)])
            _batched_ln(P, NB1, lambda j: xt[:, j, :], bnbuf, mv1, sq1,
                        rstd1, nmb1, eps_t)
            ln_v = P.val("c_v")
            for j in range(NB1):
                w = [("c_v", ln_v)]
                if j == 0:
                    w.append((osem, st_done.get(bt - 1, 0)))
                P.op("scalar", lambda s, j=j: s.activation(
                    out=h1sb[:, j, :], in_=xt[:, j, :], func=AF.Identity,
                    scale=rstd1[:, j:j + 1], bias=nmb1[:, j:j + 1]),
                    wait=w, chain=(j == 0))
            s_done[bt] = P.val("c_s")
            for j in range(NB1):
                tt = bt * NB1 + j
                P.dma("sync", lambda sy, tt=tt, j=j: sy.dma_start(
                    out=h1_d[tt * 128:(tt + 1) * 128, :], in_=h1sb[:, j, :]),
                    ssem, batch=(j > 0),
                    wait=[("c_s", s_done[bt])] if j == 0 else [])
            st_done[bt] = P.val(ssem)
            v_done[bt] = P.val("c_v")

        st_w = [("d_sa", P.val("d_sa")), ("d_sb", P.val("d_sb"))]
        for c in range(NCH):
            P.dma("sync", lambda s, c=c: s.dma_start(
                out=h1T[:, c, :], in_=h1_d[:, c * 128:(c + 1) * 128],
                transpose=True),
                "d_tr", wait=st_w if c == 0 else [], batch=(c > 0))
        h1T_tgt = P.val("d_tr")
        P.flush_block()

    if max_phase >= 2:
        _phase2_qkv(nc, P, h1T, pall, wqkv_d, qkv_d, h1T_tgt)
        h1T_scope.close()
        P.phase_barrier()
        _phase2b_attn(nc, P, pall, ones_bf, qkv_d, val_d)
    else:
        h1T_scope.close()

    if max_phase >= 3:
        P.phase_barrier()
        _phase3_outproj(nc, P, pall, wo_d, val_d, attn_d, x_d, r1_d, h2_d, eps_t)
    if max_phase >= 4:
        P.phase_barrier()
        _phase4_ffn(nc, P, pall, sgt, sut, sdt, wg_d, wu_d, wd_d, h2_d, down_d,
                    r1_d, out_d, eps_t)

    P.wait("sync", [("d_sa", P.val("d_sa")), ("d_sb", P.val("d_sb")),
                    ("d_sv", P.val("d_sv")), ("d_sg", P.val("d_sg"))])
    P.flush_block()
    es.close()
    return nc


def _phase2_qkv(nc, P, h1T, pall, wqkv_d, qkv_d, h1T_tgt):
    with nc.sbuf_tensor("wqkv_sb", [128, 2, NCH, 128], BF16) as wqkv_sb, \
         nc.sbuf_tensor("qkv_sb", [128, 2, S_TOK], BF16) as qkv_sb:
        BK = Banks()
        t_done = {}
        st_done = {0: 0, 1: 0}
        for f in range(48):
            fb = f % 2
            wsem = "d_wa" if fb == 0 else "d_wb"
            P.dma("gpsimd", lambda g, f=f, fb=fb: g.dma_start(
                out=wqkv_sb[:, fb, :, :],
                in_=wqkv_d[:, f * 128:(f + 1) * 128]
                .rearrange("(c p) x -> p c x", p=128)),
                wsem, wait=[("c_t", t_done.get(f - 2, 0))])
            w_tgt = P.val(wsem)
            ssem = "d_sa" if fb == 0 else "d_sb"
            for ts8 in range(8):
                bank = ts8 % 4
                tw = [(wsem, w_tgt)] + BK.wait_for([bank])
                if f == 0 and ts8 == 0:
                    tw.append(("d_tr", h1T_tgt))
                P.wait("tensor", tw)
                for c in range(NCH):
                    P.op("tensor", lambda t, fb=fb, c=c, ts8=ts8, bank=bank:
                         t.matmul(
                             pall[:, bank, :], wqkv_sb[:, fb, c, :],
                             h1T[:, c, ts8 * 512:(ts8 + 1) * 512],
                             start=(c == 0), stop=(c == NCH - 1),
                             skip_group_check=True),
                         inc=[("c_t", 1)] if c == NCH - 1 else ())
                tmm = P.val("c_t")
                P.op("vector", lambda v, fb=fb, ts8=ts8, bank=bank:
                     v.tensor_copy(
                         out=qkv_sb[:, fb, ts8 * 512:(ts8 + 1) * 512],
                         in_=pall[:, bank, :]),
                     wait=[("c_t", tmm), (ssem, st_done[fb])], chain=False)
                BK.set([bank], "c_v", P.val("c_v"))
            t_done[f] = P.val("c_t")
            P.dma("sync", lambda sy, f=f, fb=fb: sy.dma_start(
                out=qkv_d[f * 128:(f + 1) * 128, :], in_=qkv_sb[:, fb, :]),
                ssem, wait=[("c_v", P.val("c_v"))])
            st_done[fb] = P.val(ssem)
        P.flush_block()


def _phase2b_attn(nc, P, pall, ones_bf, qkv_d, val_d):
    NSLOT = 3 * GRP
    with nc.sbuf_tensor("qkT", [128, 2, 2, S_TOK], BF16) as qkT, \
         nc.sbuf_tensor("vTb", [128, 2, S_TOK], BF16) as vTb, \
         nc.sbuf_tensor("valsb", [128, 2, S_TOK], BF16) as valsb, \
         nc.sbuf_tensor("Ebuf", [128, NSLOT, S_TOK], BF16) as Ebuf, \
         nc.sbuf_tensor("cacc", [128, S_TOK], F32) as cacc, \
         nc.sbuf_tensor("rbc", [128, NSLOT, 128], BF16) as rbc, \
         nc.sbuf_tensor("lparts", [128, NSLOT * 4], F32) as lparts, \
         nc.sbuf_tensor("lsum", [128, NSLOT], F32) as lsum, \
         nc.sbuf_tensor("lr", [128, NSLOT], F32) as lr:

        groups = [list(range(g, g + GRP)) for g in range(0, NT, GRP)]
        NGRP = len(groups)
        items = [(h, gi, groups[gi]) for h in range(H) for gi in range(NGRP)]
        BK = Banks()
        val_store = {}
        scores_done = {}
        vmul_done = {}
        slot_free = {}
        rbc_free = {}
        lp_free = {}
        load_info = {}
        state = {"nq": 0, "pending": None}

        def emit_loads(h):
            hp = h % 2
            xsem = "d_xa" if hp == 0 else "d_xb"
            P.dma("sync", lambda sy, h=h, hp=hp: sy.dma_start(
                out=qkT[:, hp, 0, :], in_=qkv_d[h * 128:(h + 1) * 128, :]),
                xsem, wait=[("c_t", scores_done.get(h - 2, 0))])
            P.dma("sync", lambda sy, h=h, hp=hp: sy.dma_start(
                out=qkT[:, hp, 1, :],
                in_=qkv_d[(16 + h) * 128:(17 + h) * 128, :]),
                xsem, batch=True)
            qk_tgt = P.val(xsem)
            P.dma("sync", lambda sy, h=h, hp=hp: sy.dma_start(
                out=vTb[:, hp, :],
                in_=qkv_d[(32 + h) * 128:(33 + h) * 128, :]),
                xsem, wait=[("c_v", vmul_done.get(h - 2, 0))])
            load_info[h] = (xsem, qk_tgt, P.val(xsem))

        def emit_colsum_piece(nk8=2):
            if state["pending"] is None:
                return
            item, slots, k8_0, rbc_tgt = state["pending"]
            h, gi, qis = item
            first = (gi == 0)
            k8_end = min(k8_0 + nk8, 8)
            for k8 in range(k8_0, k8_end):
                pb = 6 + (k8 % 2)
                w = BK.wait_for([pb])
                if k8 == 0:
                    w = w + [("c_v", rbc_tgt)]
                P.wait("tensor", w)
                nsl = len(slots)
                for j, slot in enumerate(slots):
                    P.op("tensor", lambda t, slot=slot, k8=k8, pb=pb,
                         j=j, nsl=nsl: t.matmul(
                             pall[:, pb, :], rbc[:, slot, :],
                             Ebuf[:, slot, k8 * 512:(k8 + 1) * 512],
                             start=(j == 0), stop=(j == nsl - 1),
                             skip_group_check=True),
                         inc=[("c_t", 1)] if j == nsl - 1 else ())
                tcs = P.val("c_t")
                if first:
                    P.op("vector", lambda v, k8=k8, pb=pb: v.tensor_copy(
                        out=cacc[:, k8 * 512:(k8 + 1) * 512],
                        in_=pall[:, pb, :]),
                        wait=[("c_t", tcs)], chain=(k8 == 0))
                else:
                    P.op("vector", lambda v, k8=k8, pb=pb: v.tensor_tensor(
                        out=cacc[:, k8 * 512:(k8 + 1) * 512],
                        in0=pall[:, pb, :],
                        in1=cacc[:, k8 * 512:(k8 + 1) * 512], op=ALU.add),
                        wait=[("c_t", tcs)], chain=(k8 == 0))
                BK.set([pb], "c_v", P.val("c_v"))
            if k8_end == 8:
                cd = P.val("c_t")
                for slot in slots:
                    slot_free[slot] = cd
                    rbc_free[slot] = cd
                if gi == NGRP - 1:
                    hp = h % 2
                    xsem, _, v_tgt = load_info[h]
                    P.op("vector", lambda v, hp=hp: v.tensor_tensor(
                        out=valsb[:, hp, :], in0=vTb[:, hp, :], in1=cacc[:],
                        op=ALU.mult),
                        wait=[(xsem, v_tgt),
                              ("d_sv", val_store.get(h - 2, 0))])
                    vmul_done[h] = P.val("c_v")
                    P.dma("sync", lambda sy, h=h, hp=hp: sy.dma_start(
                        out=val_d[h * 128:(h + 1) * 128, :],
                        in_=valsb[:, hp, :]),
                        "d_sv", wait=[("c_v", vmul_done[h])])
                    val_store[h] = P.val("d_sv")
                    scores_done[h] = P.val("c_t")
                state["pending"] = None
            else:
                state["pending"] = (item, slots, k8_end, rbc_tgt)

        def emit_scores(item):
            h, gi, qis = item
            hp = h % 2
            if gi == 0:
                emit_loads(h)
            xsem, qk_tgt, _ = load_info[h]
            qT = qkT[:, hp, 0, :]
            kT = qkT[:, hp, 1, :]
            slots = [((h * NGRP + gi) % 3) * GRP + j for j in range(len(qis))]
            for slot, qi in zip(slots, qis):
                for kq in range(4):
                    b = state["nq"] % 3
                    state["nq"] += 1
                    P.wait("tensor", BK.wait_for([2 * b, 2 * b + 1])
                           + [(xsem, qk_tgt)])
                    for half in range(2):
                        P.op("tensor", lambda t, qi=qi, kq=kq, b=b,
                             half=half, qT=qT, kT=kT: t.matmul(
                                 pall[:, 2 * b + half, :],
                                 qT[:, qi * 128:(qi + 1) * 128],
                                 kT[:, kq * 1024 + half * 512:
                                    kq * 1024 + (half + 1) * 512],
                                 start=True, stop=True,
                                 skip_group_check=True),
                             inc=[("c_t", 1)] if half == 1 else ())
                    tsc = P.val("c_t")
                    ew = [("c_t", max(tsc, slot_free.get(slot, 0)))]
                    P.op("scalar", lambda s, slot=slot, kq=kq, b=b:
                         s.activation(
                             out=Ebuf[:, slot, kq * 1024:(kq + 1) * 1024],
                             in_=pall[:, 2 * b:2 * b + 2, :], func=AF.Exp,
                             scale=QSCALE),
                         wait=ew, chain=False)
                    BK.set([2 * b, 2 * b + 1], "c_s", P.val("c_s"))
                P.op("vector", lambda v, slot=slot: v.tensor_reduce(
                    out=lsum[:, slot:slot + 1],
                    in_=Ebuf[:, slot, :],
                    axis=AX.X, op=ALU.add),
                    wait=[("c_s", P.val("c_s"))], chain=False)
                P.op("vector", lambda v, slot=slot: v.reciprocal(
                    out=lr[:, slot:slot + 1], in_=lsum[:, slot:slot + 1]))
                P.op("vector", lambda v, slot=slot: v.tensor_scalar(
                    out=rbc[:, slot, :], in0=ones_bf[:],
                    scalar1=lr[:, slot:slot + 1],
                    scalar2=None, op0=ALU.mult),
                    wait=[("c_t", rbc_free.get(slot, 0))])
                emit_colsum_piece(2)
            return slots

        for item in items:
            slots = emit_scores(item)
            assert state["pending"] is None
            state["pending"] = (item, slots, 0, P.val("c_v"))
        while state["pending"] is not None:
            emit_colsum_piece(2)
        P.flush_block()


def _phase3_outproj(nc, P, pall, wo_d, val_d, attn_d, x_d, r1_d, h2_d, eps_t):
    NB = 2
    with nc.sbuf_tensor("wo_sb", [128, NCH, D], BF16) as wo_sb, \
         nc.sbuf_tensor("vslab", [128, 2, NCH, 512], BF16) as vslab, \
         nc.sbuf_tensor("attn_sb", [128, 2, NCH, 512], BF16) as attn_sb, \
         nc.sbuf_tensor("attnT", [128, NB, D], BF16) as attnT, \
         nc.sbuf_tensor("xt2", [128, NB, D], F32) as xt2, \
         nc.sbuf_tensor("y1sb", [128, NB, D], F32) as y1sb, \
         nc.sbuf_tensor("r1sb", [128, NB, D], F32) as r1sb, \
         nc.sbuf_tensor("h2sb", [128, NB, D], BF16) as h2sb, \
         nc.sbuf_tensor("bn2", [128, NB, 4, 6], F32) as bn2, \
         nc.sbuf_tensor("mvb", [128, NB, 2], F32) as mvb, \
         nc.sbuf_tensor("sqb", [128, NB], F32) as sqb, \
         nc.sbuf_tensor("rstdb", [128, NB], F32) as rstdb, \
         nc.sbuf_tensor("nmbb", [128, NB], F32) as nmbb:

        for vf in range(NCH):
            P.dma("gpsimd", lambda g, vf=vf: g.dma_start(
                out=wo_sb[:, vf, :], in_=wo_d[vf * 128:(vf + 1) * 128, :]),
                "d_wa", batch=(vf > 0))
        wo_tgt = P.val("d_wa")

        BK = Banks()
        vload = {}
        attn_store = {0: 0, 1: 0}
        fold_sdone = {}
        fold_vdone = {}
        fold_stdone = {}

        def emit_fold(fb):
            ssem = "d_wv" if fb % 2 == 0 else "d_sg"
            osem = "d_sg" if fb % 2 == 0 else "d_wv"
            xsem = "d_xa" if fb % 2 == 0 else "d_xb"
            P.wait("sync", [("c_s", fold_sdone.get(fb - 1, 0))])
            for j in range(NB):
                tt = fb * NB + j
                P.dma("sync", lambda s, tt=tt, j=j: s.dma_start(
                    out=attnT[:, j, :],
                    in_=attn_d[:, tt * 128:(tt + 1) * 128], transpose=True),
                    "d_tr", batch=(j > 0))
            tr_tgt = P.val("d_tr")
            for j in range(NB):
                tt = fb * NB + j
                P.dma("gpsimd", lambda sy, tt=tt, j=j: sy.dma_start(
                    out=xt2[:, j, :], in_=x_d[tt * 128:(tt + 1) * 128, :]),
                    xsem, batch=(j > 0),
                    wait=[("c_v", fold_vdone.get(fb - 1, 0))] if j == 0 else [])
            x_tgt = P.val(xsem)
            P.wait("vector", [("d_tr", tr_tgt)])
            _batched_ln(P, NB, lambda j: attnT[:, j, :], bn2, mvb, sqb,
                        rstdb, nmbb, eps_t)
            ln_v = P.val("c_v")
            for j in range(NB):
                P.op("scalar", lambda s, j=j: s.activation(
                    out=y1sb[:, j, :], in_=attnT[:, j, :],
                    func=AF.Identity, scale=rstdb[:, j:j + 1],
                    bias=nmbb[:, j:j + 1]),
                    wait=[("c_v", ln_v)] if j == 0 else [], chain=(j == 0))
            y1_s = P.val("c_s")
            for j in range(NB):
                P.op("vector", lambda v, j=j: v.tensor_tensor(
                    out=r1sb[:, j, :], in0=y1sb[:, j, :],
                    in1=xt2[:, j, :], op=ALU.add),
                    wait=[("c_s", y1_s), (xsem, x_tgt),
                          (osem, fold_stdone.get(fb - 1, 0))]
                    if j == 0 else [], chain=(j == 0))
            r1_v = P.val("c_v")
            for j in range(NB):
                tt = fb * NB + j
                P.dma("sync", lambda sy, tt=tt, j=j: sy.dma_start(
                    out=r1_d[tt * 128:(tt + 1) * 128, :], in_=r1sb[:, j, :]),
                    ssem, batch=(j > 0),
                    wait=[("c_v", r1_v)] if j == 0 else [])
            _batched_ln(P, NB, lambda j: r1sb[:, j, :], bn2, mvb, sqb,
                        rstdb, nmbb, eps_t)
            ln2_v = P.val("c_v")
            for j in range(NB):
                P.op("scalar", lambda s, j=j: s.activation(
                    out=h2sb[:, j, :], in_=r1sb[:, j, :],
                    func=AF.Identity, scale=rstdb[:, j:j + 1],
                    bias=nmbb[:, j:j + 1]),
                    wait=[("c_v", ln2_v),
                          (osem, fold_stdone.get(fb - 1, 0))]
                    if j == 0 else [], chain=(j == 0))
            fold_sdone[fb] = P.val("c_s")
            for j in range(NB):
                tt = fb * NB + j
                P.dma("sync", lambda sy, tt=tt, j=j: sy.dma_start(
                    out=h2_d[tt * 128:(tt + 1) * 128, :], in_=h2sb[:, j, :]),
                    ssem, batch=(j > 0),
                    wait=[("c_s", fold_sdone[fb])] if j == 0 else [])
            fold_vdone[fb] = P.val("c_v")
            fold_stdone[fb] = P.val(ssem)

        for tc in range(NTC):
            sb = tc % 2
            vb = tc % 2
            xsem = ("d_sv", "d_xb")[vb]
            P.dma("sync", lambda sy, tc=tc, vb=vb: sy.dma_start(
                out=vslab[:, vb, 0:8, :],
                in_=val_d[0:1024, tc * 512:(tc + 1) * 512]
                .rearrange("(v p) t -> p v t", p=128)),
                xsem, wait=[("c_t", vload.get(tc - 2, 0))])
            P.dma("gpsimd", lambda sc, tc=tc, vb=vb: sc.dma_start(
                out=vslab[:, vb, 8:16, :],
                in_=val_d[1024:2048, tc * 512:(tc + 1) * 512]
                .rearrange("(v p) t -> p v t", p=128)),
                xsem, batch=True, wait=[("c_t", vload.get(tc - 2, 0))])
            vl_tgt = P.val(xsem)
            for fo in range(NCH):
                bank = fo % 4
                tw = [(xsem, vl_tgt)] + BK.wait_for([bank])
                if tc == 0 and fo == 0:
                    tw.append(("d_wa", wo_tgt))
                P.wait("tensor", tw)
                for vf in range(NCH):
                    P.op("tensor", lambda t, vf=vf, fo=fo, vb=vb, bank=bank:
                         t.matmul(
                             pall[:, bank, :],
                             wo_sb[:, vf, fo * 128:(fo + 1) * 128],
                             vslab[:, vb, vf, :],
                             start=(vf == 0), stop=(vf == NCH - 1),
                             skip_group_check=True),
                         inc=[("c_t", 1)] if vf == NCH - 1 else ())
                tpo = P.val("c_t")
                if fo == NCH - 1:
                    vload[tc] = tpo
                ssem2 = "d_sa" if sb == 0 else "d_sb"
                P.op("scalar", lambda s, fo=fo, sb=sb, bank=bank: s.activation(
                    out=attn_sb[:, sb, fo, :], in_=pall[:, bank, :],
                    func=AF.Copy),
                    wait=[("c_t", tpo), (ssem2, attn_store[sb])],
                    chain=False)
                BK.set([bank], "c_s", P.val("c_s"))
            attn_evac = P.val("c_s")
            P.dma("sync", lambda sy, tc=tc, sb=sb: sy.dma_start(
                out=attn_d[0:1024, tc * 512:(tc + 1) * 512]
                .rearrange("(f p) t -> p f t", p=128),
                in_=attn_sb[:, sb, 0:8, :]),
                ssem2, wait=[("c_s", attn_evac)])
            P.dma("gpsimd", lambda sc, tc=tc, sb=sb: sc.dma_start(
                out=attn_d[1024:2048, tc * 512:(tc + 1) * 512]
                .rearrange("(f p) t -> p f t", p=128),
                in_=attn_sb[:, sb, 8:16, :]),
                ssem2, batch=True, wait=[("c_s", attn_evac)])
            attn_store[sb] = P.val(ssem2)
            attn_ready = P.val(ssem2)
            if tc >= 1:
                psem = "d_sa" if (tc - 1) % 2 == 0 else "d_sb"
                P.wait("sync", [(psem, attn_store[(tc - 1) % 2])])
                emit_fold(2 * (tc - 1))
                emit_fold(2 * (tc - 1) + 1)
        psem = "d_sa" if (NTC - 1) % 2 == 0 else "d_sb"
        P.wait("sync", [(psem, attn_store[(NTC - 1) % 2])])
        emit_fold(2 * (NTC - 1))
        emit_fold(2 * (NTC - 1) + 1)
        P.flush_block()


def _phase4_ffn(nc, P, pall, sgt, sut, sdt, wg_d, wu_d, wd_d, h2_d, down_d,
                r1_d, out_d, eps_t):
    with nc.sbuf_tensor("h2T", [128, 2, NCH, 512], BF16) as h2T, \
         nc.sbuf_tensor("wg_sb", [128, 3, NCH, 256], BF16) as wg_sb, \
         nc.sbuf_tensor("wu_sb", [128, 3, NCH, 256], BF16) as wu_sb, \
         nc.sbuf_tensor("wd_sb", [128, 2, 2, D], BF16) as wd_sb, \
         nc.sbuf_tensor("fuse", [128, NI, 512], BF16) as fuse, \
         nc.sbuf_tensor("sgl", [128, 2, 512], BF16) as sgl, \
         nc.sbuf_tensor("down_sb", [128, 2, 2, 512], BF16) as down_sb, \
         nc.sbuf_tensor("downT", [128, 4, D], BF16) as downT, \
         nc.sbuf_tensor("r1t", [128, D], F32) as r1t, \
         nc.sbuf_tensor("y2sb", [128, D], F32) as y2sb, \
         nc.sbuf_tensor("outsb", [128, D], F32) as outsb, \
         nc.sbuf_tensor("bn3", [128, 1, 4, 6], F32) as bn3, \
         nc.sbuf_tensor("mvb3", [128, 4, 2], F32) as mvb3, \
         nc.sbuf_tensor("sqb3", [128, 4], F32) as sqb3, \
         nc.sbuf_tensor("rstdb3", [128, 4], F32) as rstdb3, \
         nc.sbuf_tensor("nmbb3", [128, 4], F32) as nmbb3:

        NHG = 32
        BK = Banks()
        gu_done = {}
        wgu_free = {0: 0, 1: 0, 2: 0}
        wd_free = {0: 0, 1: 0}
        sgl_free = {0: 0, 1: 0}
        fuse_done = {}
        down_read = {}
        down_store_tc = {}
        down_store2 = {}

        preloaded = {}
        gctr = {"n": 0}
        epi = {"apply_v": 0, "stats_v": 0, "st": 0, "tr": 0}

        def load_wgu(hg):
            gb = gctr["n"] % 3
            gctr["n"] += 1
            P.dma("gpsimd", lambda gg, hg=hg, gb=gb: gg.dma_start(
                out=wg_sb[:, gb, :, :],
                in_=wg_d[:, hg * 256:(hg + 1) * 256]
                .rearrange("(c p) i -> p c i", p=128)),
                "d_wa", wait=[("c_t", wgu_free[gb])])
            P.dma("gpsimd", lambda gg, hg=hg, gb=gb: gg.dma_start(
                out=wu_sb[:, gb, :, :],
                in_=wu_d[:, hg * 256:(hg + 1) * 256]
                .rearrange("(c p) i -> p c i", p=128)), "d_wa", batch=True)
            return "d_wa", P.val("d_wa"), gb

        def emit_epilogue_front(tc):
            for j in range(4):
                tt = tc * 4 + j
                w = ([("d_sg", down_store_tc[tc]),
                      ("c_v", epi["apply_v"])] if j == 0 else [])
                P.dma("scalar", lambda s, tt=tt, j=j: s.dma_start(
                    out=downT[:, j, :],
                    in_=down_d[:, tt * 128:(tt + 1) * 128],
                    transpose=True), "d_xa", wait=w, batch=(j > 0))
            epi["tr"] = P.val("d_xa")
            P.wait("vector", [("d_xa", epi["tr"])])
            for j in range(4):
                for q in range(4):
                    P.op("vector", lambda v, j=j, q=q: v.bn_stats(
                        out=bn3[:, 0, q, :],
                        in_=downT[:, j, q * 512:(q + 1) * 512]),
                        chain=(j == 0 and q == 0))
                P.op("vector", lambda v, j=j: v.bn_aggr(
                    out=mvb3[:, j, :], in_=bn3[:, 0, :, :]))
            epi["stats_v"] = P.val("c_v")

        def emit_epilogue_back(tc):
            P.op("scalar", lambda s: s.activation(
                out=sqb3[:], in_=mvb3[:, :, 1], func=AF.Sqrt, bias=eps_t[:],
                scale=1.0), wait=[("c_v", epi["stats_v"])])
            sq_s = P.val("c_s")
            P.op("vector", lambda v: v.reciprocal(out=rstdb3[:], in_=sqb3[:]),
                 wait=[("c_s", sq_s)])
            P.op("vector", lambda v: v.scalar_tensor_tensor(
                out=nmbb3[:], in0=mvb3[:, :, 0], scalar=-1.0, in1=rstdb3[:],
                op0=ALU.mult, op1=ALU.mult))
            for j in range(4):
                tt = tc * 4 + j
                P.dma("sync", lambda sy, tt=tt: sy.dma_start(
                    out=r1t[:], in_=r1_d[tt * 128:(tt + 1) * 128, :]),
                    "d_xb", wait=[("c_v", epi["apply_v"])])
                r1_tgt = P.val("d_xb")
                P.op("vector", lambda v, j=j: v.tensor_scalar(
                    out=y2sb[:], in0=r1t[:], scalar1=nmbb3[:, j:j + 1],
                    scalar2=None, op0=ALU.add),
                    wait=[("d_xb", r1_tgt), ("d_sa", epi["st"])])
                P.op("vector", lambda v, j=j: v.scalar_tensor_tensor(
                    out=outsb[:], in0=downT[:, j, :],
                    scalar=rstdb3[:, j:j + 1], in1=y2sb[:],
                    op0=ALU.mult, op1=ALU.add))
                epi["apply_v"] = P.val("c_v")
                P.dma("sync", lambda sy, tt=tt: sy.dma_start(
                    out=out_d[tt * 128:(tt + 1) * 128, :], in_=outsb[:]),
                    "d_sa", wait=[("c_v", epi["apply_v"])])
                epi["st"] = P.val("d_sa")

        def emit_h2T(tc):
            tb = tc % 2
            for c in range(NCH):
                tr_w = ([("d_tr", P.val("d_tr")),
                         ("c_t", gu_done.get(tc - 2, 0))] if c == 0 else [])
                P.dma("sync", lambda s, tc=tc, tb=tb, c=c: s.dma_start(
                    out=h2T[:, tb, c, :],
                    in_=h2_d[tc * 512:(tc + 1) * 512, c * 128:(c + 1) * 128],
                    transpose=True),
                    "d_tr", wait=tr_w, batch=(c > 0))
            return P.val("d_tr")

        h2T_tgts = {}
        for tc in range(NTC):
            tb = tc % 2
            if tc not in h2T_tgts:
                h2T_tgts[tc] = emit_h2T(tc)
            h2T_tgt = h2T_tgts[tc]
            if tc >= 1:
                emit_epilogue_front(tc - 1)

            for hg in range(NHG):
                for la in (hg, hg + 1, hg + 2):
                    if la < NHG and (tc, la) not in preloaded:
                        preloaded[(tc, la)] = load_wgu(la)
                gsem, wg_tgt, gb = preloaded.pop((tc, hg))
                for i2 in range(2):
                    i = hg * 2 + i2
                    pg = (0, 1, 4)[i % 3]
                    pu = (2, 3, 5)[i % 3]
                    P.wait("tensor", [(gsem, wg_tgt), ("d_tr", h2T_tgt)]
                           + BK.wait_for([pg, pu]))
                    for c in range(NCH):
                        P.op("tensor", lambda t, c=c, i2=i2, gb=gb, tb=tb,
                             pg=pg: t.matmul(
                                 pall[:, pg, :],
                                 wg_sb[:, gb, c, i2 * 128:(i2 + 1) * 128],
                                 h2T[:, tb, c, :],
                                 start=(c == 0), stop=(c == NCH - 1),
                                 skip_group_check=True),
                             inc=[("c_t", 1)] if c == NCH - 1 else ())
                    tg = P.val("c_t")
                    for c in range(NCH):
                        P.op("tensor", lambda t, c=c, i2=i2, gb=gb, tb=tb,
                             pu=pu: t.matmul(
                                 pall[:, pu, :],
                                 wu_sb[:, gb, c, i2 * 128:(i2 + 1) * 128],
                                 h2T[:, tb, c, :],
                                 start=(c == 0), stop=(c == NCH - 1),
                                 skip_group_check=True),
                             inc=[("c_t", 1)] if c == NCH - 1 else ())
                    tu = P.val("c_t")
                    P.op("scalar", lambda s, i=i, pg=pg: s.activation(
                        out=sgl[:, i % 2, :], in_=pall[:, pg, :],
                        func=AF.Silu, scale=sgt[:, i:i + 1]),
                        wait=[("c_t", tg), ("c_v", sgl_free[i % 2])],
                        chain=False)
                    BK.set([pg], "c_s", P.val("c_s"))
                    P.op("vector", lambda v, i=i, pu=pu: v.scalar_tensor_tensor(
                        out=fuse[:, i, :], in0=pall[:, pu, :],
                        scalar=sut[:, i:i + 1], in1=sgl[:, i % 2, :],
                        op0=ALU.mult, op1=ALU.mult),
                        wait=[("c_t", tu), ("c_s", P.val("c_s")),
                              ("c_t", down_read.get(tc - 1, 0))],
                        chain=False)
                    BK.set([pu], "c_v", P.val("c_v"))
                    sgl_free[i % 2] = P.val("c_v")
                wgu_free[gb] = P.val("c_t")
            gu_done[tc] = P.val("c_t")
            fuse_done[tc] = P.val("c_v")
            if tc >= 1:
                emit_epilogue_back(tc - 1)
            if tc + 1 < NTC:
                for hg in range(2):
                    preloaded[(tc + 1, hg)] = load_wgu(hg)

            for p in range(2):
                if p == 1 and tc + 1 < NTC:
                    h2T_tgts[tc + 1] = emit_h2T(tc + 1)
                for s2 in range(32):
                    wdb = s2 % 2
                    dsem = "d_wb" if wdb == 0 else "d_wv"
                    P.dma("gpsimd", lambda gg, s2=s2, wdb=wdb: gg.dma_start(
                        out=wd_sb[:, wdb, :, :],
                        in_=wd_d[s2 * 256:(s2 + 1) * 256, :]
                        .rearrange("(q p) f -> p q f", p=128)),
                        dsem, wait=[("c_t", wd_free[wdb])])
                    wd_tgt = P.val(dsem)
                    tw = [(dsem, wd_tgt), ("c_v", fuse_done[tc])]
                    if s2 == 0:
                        tw += BK.wait_for(list(range(8)))
                        tw.append(("c_s", P.val("c_s")))
                    P.wait("tensor", tw)
                    for q in range(2):
                        i = s2 * 2 + q
                        for f8 in range(8):
                            fo = p * 8 + f8
                            P.op("tensor", lambda t, q=q, f8=f8, fo=fo, i=i,
                                 wdb=wdb: t.matmul(
                                     pall[:, f8, :],
                                     wd_sb[:, wdb, q, fo * 128:(fo + 1) * 128],
                                     fuse[:, i, :],
                                     start=(i == 0), stop=(i == NI - 1),
                                     skip_group_check=True),
                                 inc=[("c_t", 1)]
                                 if (q == 1 and f8 == 7) else ())
                    wd_free[wdb] = P.val("c_t")
                tdn = P.val("c_t")
                for grp in range(4):
                    hb = grp % 2
                    for f2 in range(2):
                        f8 = grp * 2 + f2
                        fo = p * 8 + f8
                        P.op("scalar", lambda s, f2=f2, f8=f8, fo=fo,
                             hb=hb: s.activation(
                                 out=down_sb[:, hb, f2, :],
                                 in_=pall[:, f8, :],
                                 func=AF.Copy, scale=sdt[:, fo:fo + 1]),
                             wait=[("c_t", tdn),
                                   ("d_sg", down_store2.get(hb, 0))],
                             chain=False)
                    BK.set([grp * 2, grp * 2 + 1], "c_s", P.val("c_s"))
                    P.dma("sync", lambda sy, tc=tc, p=p, grp=grp, hb=hb:
                          sy.dma_start(
                              out=down_d[(p * 8 + grp * 2) * 128:
                                         (p * 8 + grp * 2 + 2) * 128,
                                         tc * 512:(tc + 1) * 512]
                              .rearrange("(f q) t -> q f t", q=128),
                              in_=down_sb[:, hb, :, :]),
                          "d_sg", wait=[("c_s", P.val("c_s"))])
                    down_store2[hb] = P.val("d_sg")
            down_store_tc[tc] = P.val("d_sg")
            down_read[tc] = P.val("c_t")
        emit_epilogue_front(NTC - 1)
        emit_epilogue_back(NTC - 1)
        P.flush_block()


def kernel(L0_attn_x, L0_attn_w_qkv, L0_attn_w_o, L0_ffn_w_gate, L0_ffn_w_up,
           L0_ffn_w_down, L0_ffn_s_gate, L0_ffn_s_up, L0_ffn_s_down):
    nc = build()
    x = np.ascontiguousarray(np.asarray(L0_attn_x, dtype=np.float32))
    shared = {
        "wqkv": np.ascontiguousarray(np.asarray(L0_attn_w_qkv, dtype=np.float32)),
        "wo": np.ascontiguousarray(np.asarray(L0_attn_w_o, dtype=np.float32)),
        "wg": np.ascontiguousarray(np.asarray(L0_ffn_w_gate, dtype=np.int8)),
        "wu": np.ascontiguousarray(np.asarray(L0_ffn_w_up, dtype=np.int8)),
        "wd": np.ascontiguousarray(np.asarray(L0_ffn_w_down, dtype=np.int8)),
        "sg": np.ascontiguousarray(np.asarray(L0_ffn_s_gate, dtype=np.float32).reshape(NI, 128).T),
        "su": np.ascontiguousarray(np.asarray(L0_ffn_s_up, dtype=np.float32).reshape(NI, 128).T),
        "sd": np.ascontiguousarray(np.asarray(L0_ffn_s_down, dtype=np.float32).reshape(NCH, 128).T),
    }
    in_maps = [dict(shared, x=np.ascontiguousarray(x[b])) for b in range(8)]
    res = run_bass_kernel_spmd(nc, in_maps, core_ids=list(range(8)))
    return np.stack([res.results[b]["out"] for b in range(8)], axis=0)


# version 22
# speedup vs baseline: 1.1935x; 1.1935x over previous
import numpy as np
from contextlib import ExitStack

import concourse.bass as bass
import concourse.mybir as mybir
from concourse.bass_utils import run_bass_kernel_spmd

F32 = mybir.dt.float32
BF16 = mybir.dt.bfloat16
I8 = mybir.dt.int8
AF = mybir.ActivationFunctionType
ALU = mybir.AluOpType
AX = mybir.AxisListType

S_TOK = 4096
D = 2048
H = 16
INTER = 8192
NT = S_TOK // 128
NCH = D // 128
NI = INTER // 128
NTC = 8
EPS = 1e-5
QSCALE = float(1.0 / np.sqrt(128.0))
GRP = 4

ENGS = ("tensor", "scalar", "vector", "sync", "gpsimd")
SEM_NAMES = ("d_xa", "d_xb", "d_sa", "d_sb", "d_sv", "d_sg", "d_wa", "d_wb",
             "d_wv", "d_tr", "c_t", "c_s", "c_v", "c_g")


class Prog:
    def __init__(self, nc):
        self.nc = nc
        self.ops = {e: [] for e in ENGS}
        self.cnt = {s: 0 for s in SEM_NAMES}
        self.stack = ExitStack()
        self.S = {s: self.stack.enter_context(nc.semaphore(s)) for s in SEM_NAMES}

    def _emit(self, eng, fn, waits, incs):
        S = self.S
        waits = tuple((s, t) for s, t in waits if t > 0)
        incs = tuple(incs)

        def run(e):
            for s, t in waits:
                e.wait_ge(S[s], t)
            ins = fn(e)
            for s, n in incs:
                ins.then_inc(S[s], n)
            return ins

        self.ops[eng].append(run)

    def op(self, eng, fn, wait=(), inc=(), chain=True):
        waits = list(wait)
        incs = list(inc)
        if eng == "scalar":
            assert not incs
            if chain:
                waits.append(("c_s", self.cnt["c_s"]))
            self.cnt["c_s"] += 1
            incs.append(("c_s", 1))
        elif eng == "vector":
            assert not incs
            if chain:
                waits.append(("c_v", self.cnt["c_v"]))
            self.cnt["c_v"] += 1
            incs.append(("c_v", 1))
        else:
            for s, n in incs:
                self.cnt[s] += n
        self._emit(eng, fn, waits, incs)

    def dma(self, eng, fn, sem, wait=(), batch=False):
        waits = list(wait)
        if not batch:
            waits.append((sem, self.cnt[sem]))
        self.cnt[sem] += 16
        self._emit(eng, fn, waits, [(sem, 16)])

    def wait(self, eng, wait):
        S = self.S
        waits = tuple((s, t) for s, t in wait if t > 0)
        if not waits:
            return

        def run(e):
            for s, t in waits:
                e.wait_ge(S[s], t)

        self.ops[eng].append(run)

    def val(self, s):
        return self.cnt[s]

    def phase_barrier(self):
        w = [(s, self.cnt[s]) for s in SEM_NAMES]
        for e in ENGS:
            self.wait(e, w)

    def flush_block(self):
        ops = self.ops
        with self.nc.Block() as b:
            @b.tensor
            def _(t):
                for f in ops["tensor"]:
                    f(t)

            @b.scalar
            def _(s):
                for f in ops["scalar"]:
                    f(s)

            @b.vector
            def _(v):
                for f in ops["vector"]:
                    f(v)

            @b.sync
            def _(sy):
                for f in ops["sync"]:
                    f(sy)

            @b.gpsimd
            def _(g):
                for f in ops["gpsimd"]:
                    f(g)
        self.ops = {e: [] for e in ENGS}



def _batched_ln(P, nb, tiles_ap, bnbuf, mvb, sqb, rstdb, nmbb, eps_t):
    for j in range(nb):
        for q in range(4):
            P.op("vector", lambda v, j=j, q=q, t=tiles_ap: v.bn_stats(
                out=bnbuf[:, j, q, :], in_=t(j)[:, q * 512:(q + 1) * 512]),
                chain=(j == 0 and q == 0))
    for j in range(nb):
        P.op("vector", lambda v, j=j: v.bn_aggr(
            out=mvb[:, j, :], in_=bnbuf[:, j, :, :]), chain=(j == 0))
    P.op("scalar", lambda s: s.activation(
        out=sqb[:], in_=mvb[:, :, 1], func=AF.Sqrt, bias=eps_t[:], scale=1.0),
        wait=[("c_v", P.val("c_v"))])
    P.op("vector", lambda v: v.reciprocal(out=rstdb[:], in_=sqb[:]),
         wait=[("c_s", P.val("c_s"))])
    P.op("vector", lambda v: v.scalar_tensor_tensor(
        out=nmbb[:], in0=mvb[:, :, 0], scalar=-1.0, in1=rstdb[:],
        op0=ALU.mult, op1=ALU.mult))


def _ln_stats(P, tile_ap, bnbuf, mv):
    for j in range(4):
        P.op("vector", lambda v, t=tile_ap, b=bnbuf, j=j: v.bn_stats(
            out=b[:, j, :], in_=t[:, j * 512:(j + 1) * 512]))
    P.op("vector", lambda v, b=bnbuf, m=mv: v.bn_aggr(out=m[:], in_=b[:]))


def _ln_scalebias(P, mv, sq, rstd, nmb, eps_t):
    P.op("scalar", lambda s, m=mv, q=sq, e=eps_t: s.activation(
        out=q[:], in_=m[:, 1:2], func=AF.Sqrt, bias=e[:], scale=1.0),
        wait=[("c_v", P.val("c_v"))])
    P.op("vector", lambda v, q=sq, r=rstd: v.reciprocal(out=r[:], in_=q[:]),
         wait=[("c_s", P.val("c_s"))])
    P.op("vector", lambda v, m=mv, r=rstd, o=nmb: v.tensor_scalar(
        out=o[:], in0=m[:, 0:1], scalar1=r[:], scalar2=-1.0,
        op0=ALU.mult, op1=ALU.mult))


class Banks:

    def __init__(self):
        self.free = {b: ("c_s", 0) for b in range(8)}

    def wait_for(self, banks):
        return [self.free[b] for b in banks]

    def set(self, banks, sem, tgt):
        for b in banks:
            self.free[b] = (sem, tgt)


def build(debug=0, max_phase=5):
    nc = bass.Bass("TRN2", target_bir_lowering=False)

    x_d = nc.dram_tensor("x", [S_TOK, D], F32, kind="ExternalInput")
    wqkv_d = nc.dram_tensor("wqkv", [D, 3 * D], F32, kind="ExternalInput")
    wo_d = nc.dram_tensor("wo", [D, D], F32, kind="ExternalInput")
    wg_d = nc.dram_tensor("wg", [D, INTER], I8, kind="ExternalInput")
    wu_d = nc.dram_tensor("wu", [D, INTER], I8, kind="ExternalInput")
    wd_d = nc.dram_tensor("wd", [INTER, D], I8, kind="ExternalInput")
    sg_d = nc.dram_tensor("sg", [128, NI], F32, kind="ExternalInput")
    su_d = nc.dram_tensor("su", [128, NI], F32, kind="ExternalInput")
    sd_d = nc.dram_tensor("sd", [128, NCH], F32, kind="ExternalInput")
    out_d = nc.dram_tensor("out", [S_TOK, D], F32, kind="ExternalOutput")

    sk = "ExternalOutput" if debug else "Internal"
    h1_d = nc.dram_tensor("h1d", [S_TOK, D], BF16, kind=sk)
    val_d = nc.dram_tensor("vald", [D, S_TOK], BF16, kind=sk)
    attn_d = nc.dram_tensor("attnd", [D, S_TOK], BF16, kind=sk)
    r1_d = nc.dram_tensor("r1d", [S_TOK, D], F32, kind=sk)
    h2_d = nc.dram_tensor("h2d", [S_TOK, D], BF16, kind=sk)
    down_d = nc.dram_tensor("downd", [D, S_TOK], BF16, kind=sk)
    qkv_d = nc.dram_tensor("qkvd", [3 * D, S_TOK], BF16, kind=sk)

    es = ExitStack()
    P = Prog(nc)
    es.enter_context(P.stack)

    sgt = es.enter_context(nc.sbuf_tensor("sgt", [128, NI], F32))
    sut = es.enter_context(nc.sbuf_tensor("sut", [128, NI], F32))
    sdt = es.enter_context(nc.sbuf_tensor("sdt", [128, NCH], F32))
    ones_bf = es.enter_context(nc.sbuf_tensor("ones_bf", [128, 128], BF16))
    eps_t = es.enter_context(nc.sbuf_tensor("eps_t", [128, 1], F32))
    pall = es.enter_context(nc.psum_tensor("pall", [128, 8, 512], F32))

    P.dma("gpsimd", lambda g: g.dma_start(
        out=sgt[:], in_=sg_d[:]), "d_wa")
    P.dma("gpsimd", lambda g: g.dma_start(
        out=sut[:], in_=su_d[:]), "d_wa")
    P.dma("gpsimd", lambda g: g.dma_start(
        out=sdt[:], in_=sd_d[:]), "d_wa")
    P.op("gpsimd", lambda g: g.memset(ones_bf[:], 1.0), inc=[("c_g", 1)])
    P.op("gpsimd", lambda g: g.memset(eps_t[:], EPS), inc=[("c_g", 1)])
    for _e in ("scalar", "vector", "tensor"):
        P.wait(_e, [("c_g", 2)])

    h1T_scope = ExitStack()
    h1T = h1T_scope.enter_context(nc.sbuf_tensor("h1T", [128, NCH, S_TOK], BF16))

    NB1 = 4
    with nc.sbuf_tensor("xt", [128, NB1, D], F32) as xt, \
         nc.sbuf_tensor("h1sb", [128, NB1, D], BF16) as h1sb, \
         nc.sbuf_tensor("bnbuf", [128, NB1, 4, 6], F32) as bnbuf, \
         nc.sbuf_tensor("mv1", [128, NB1, 2], F32) as mv1, \
         nc.sbuf_tensor("sq1", [128, NB1], F32) as sq1, \
         nc.sbuf_tensor("rstd1", [128, NB1], F32) as rstd1, \
         nc.sbuf_tensor("nmb1", [128, NB1], F32) as nmb1:

        nbat = NT // NB1
        s_done = {}
        v_done = {}
        st_done = {}
        for bt in range(nbat):
            xsem = "d_xa" if bt % 2 == 0 else "d_xb"
            ssem = "d_sa" if bt % 2 == 0 else "d_sb"
            osem = "d_sb" if bt % 2 == 0 else "d_sa"
            for j in range(NB1):
                tt = bt * NB1 + j
                P.dma("sync", lambda sy, tt=tt, j=j: sy.dma_start(
                    out=xt[:, j, :], in_=x_d[tt * 128:(tt + 1) * 128, :]),
                    xsem, batch=(j > 0),
                    wait=[("c_s", s_done.get(bt - 1, 0))] if j == 0 else [])
            x_tgt = P.val(xsem)
            P.wait("vector", [(xsem, x_tgt), ("c_g", 2)])
            _batched_ln(P, NB1, lambda j: xt[:, j, :], bnbuf, mv1, sq1,
                        rstd1, nmb1, eps_t)
            ln_v = P.val("c_v")
            for j in range(NB1):
                w = [("c_v", ln_v)]
                if j == 0:
                    w.append((osem, st_done.get(bt - 1, 0)))
                P.op("scalar", lambda s, j=j: s.activation(
                    out=h1sb[:, j, :], in_=xt[:, j, :], func=AF.Identity,
                    scale=rstd1[:, j:j + 1], bias=nmb1[:, j:j + 1]),
                    wait=w, chain=(j == 0))
            s_done[bt] = P.val("c_s")
            for j in range(NB1):
                tt = bt * NB1 + j
                P.dma("sync", lambda sy, tt=tt, j=j: sy.dma_start(
                    out=h1_d[tt * 128:(tt + 1) * 128, :], in_=h1sb[:, j, :]),
                    ssem, batch=(j > 0),
                    wait=[("c_s", s_done[bt])] if j == 0 else [])
            st_done[bt] = P.val(ssem)
            v_done[bt] = P.val("c_v")

        st_w = [("d_sa", P.val("d_sa")), ("d_sb", P.val("d_sb"))]
        for c in range(NCH):
            P.dma("sync", lambda s, c=c: s.dma_start(
                out=h1T[:, c, :], in_=h1_d[:, c * 128:(c + 1) * 128],
                transpose=True),
                "d_tr", wait=st_w if c == 0 else [], batch=(c > 0))
        h1T_tgt = P.val("d_tr")
        P.flush_block()

    if max_phase >= 2:
        _phase2_qkv(nc, P, h1T, pall, wqkv_d, qkv_d, h1T_tgt)
        h1T_scope.close()
        P.phase_barrier()
        _phase2b_attn(nc, P, pall, ones_bf, qkv_d, val_d)
    else:
        h1T_scope.close()

    if max_phase >= 3:
        P.phase_barrier()
        _phase3_outproj(nc, P, pall, wo_d, val_d, attn_d, x_d, r1_d, h2_d, eps_t)
    if max_phase >= 4:
        P.phase_barrier()
        _phase4_ffn(nc, P, pall, sgt, sut, sdt, wg_d, wu_d, wd_d, h2_d, down_d,
                    r1_d, out_d, eps_t)

    P.wait("sync", [("d_sa", P.val("d_sa")), ("d_sb", P.val("d_sb")),
                    ("d_sv", P.val("d_sv")), ("d_sg", P.val("d_sg"))])
    P.flush_block()
    es.close()
    return nc


def _phase2_qkv(nc, P, h1T, pall, wqkv_d, qkv_d, h1T_tgt):
    with nc.sbuf_tensor("wqkv_sb", [128, 2, NCH, 128], BF16) as wqkv_sb, \
         nc.sbuf_tensor("qkv_sb", [128, 2, S_TOK], BF16) as qkv_sb:
        BK = Banks()
        t_done = {}
        st_done = {0: 0, 1: 0}
        for f in range(48):
            fb = f % 2
            wsem = "d_wa" if fb == 0 else "d_wb"
            P.dma("gpsimd", lambda g, f=f, fb=fb: g.dma_start(
                out=wqkv_sb[:, fb, :, :],
                in_=wqkv_d[:, f * 128:(f + 1) * 128]
                .rearrange("(c p) x -> p c x", p=128)),
                wsem, wait=[("c_t", t_done.get(f - 2, 0))])
            w_tgt = P.val(wsem)
            ssem = "d_sa" if fb == 0 else "d_sb"
            for ts8 in range(8):
                bank = ts8 % 4
                tw = [(wsem, w_tgt)] + BK.wait_for([bank])
                if f == 0 and ts8 == 0:
                    tw.append(("d_tr", h1T_tgt))
                P.wait("tensor", tw)
                for c in range(NCH):
                    P.op("tensor", lambda t, fb=fb, c=c, ts8=ts8, bank=bank:
                         t.matmul(
                             pall[:, bank, :], wqkv_sb[:, fb, c, :],
                             h1T[:, c, ts8 * 512:(ts8 + 1) * 512],
                             start=(c == 0), stop=(c == NCH - 1),
                             skip_group_check=True),
                         inc=[("c_t", 1)] if c == NCH - 1 else ())
                tmm = P.val("c_t")
                P.op("vector", lambda v, fb=fb, ts8=ts8, bank=bank:
                     v.tensor_copy(
                         out=qkv_sb[:, fb, ts8 * 512:(ts8 + 1) * 512],
                         in_=pall[:, bank, :]),
                     wait=[("c_t", tmm), (ssem, st_done[fb])], chain=False)
                BK.set([bank], "c_v", P.val("c_v"))
            t_done[f] = P.val("c_t")
            P.dma("sync", lambda sy, f=f, fb=fb: sy.dma_start(
                out=qkv_d[f * 128:(f + 1) * 128, :], in_=qkv_sb[:, fb, :]),
                ssem, wait=[("c_v", P.val("c_v"))])
            st_done[fb] = P.val(ssem)
        P.flush_block()


def _phase2b_attn(nc, P, pall, ones_bf, qkv_d, val_d):
    NSLOT = 3 * GRP
    with nc.sbuf_tensor("qkT", [128, 2, 2, S_TOK], BF16) as qkT, \
         nc.sbuf_tensor("vTb", [128, 2, S_TOK], BF16) as vTb, \
         nc.sbuf_tensor("valsb", [128, 2, S_TOK], BF16) as valsb, \
         nc.sbuf_tensor("Ebuf", [128, NSLOT, S_TOK], BF16) as Ebuf, \
         nc.sbuf_tensor("cacc", [128, S_TOK], F32) as cacc, \
         nc.sbuf_tensor("rbc", [128, NSLOT, 128], BF16) as rbc, \
         nc.sbuf_tensor("lparts", [128, NSLOT * 4], F32) as lparts, \
         nc.sbuf_tensor("lsum", [128, NSLOT], F32) as lsum, \
         nc.sbuf_tensor("lr", [128, NSLOT], F32) as lr:

        groups = [list(range(g, g + GRP)) for g in range(0, NT, GRP)]
        NGRP = len(groups)
        items = [(h, gi, groups[gi]) for h in range(H) for gi in range(NGRP)]
        BK = Banks()
        val_store = {}
        scores_done = {}
        vmul_done = {}
        slot_free = {}
        rbc_free = {}
        lp_free = {}
        load_info = {}
        state = {"nq": 0, "pending": None}

        def emit_loads(h):
            hp = h % 2
            xsem = "d_xa" if hp == 0 else "d_xb"
            P.dma("sync", lambda sy, h=h, hp=hp: sy.dma_start(
                out=qkT[:, hp, 0, :], in_=qkv_d[h * 128:(h + 1) * 128, :]),
                xsem, wait=[("c_t", scores_done.get(h - 2, 0))])
            P.dma("sync", lambda sy, h=h, hp=hp: sy.dma_start(
                out=qkT[:, hp, 1, :],
                in_=qkv_d[(16 + h) * 128:(17 + h) * 128, :]),
                xsem, batch=True)
            qk_tgt = P.val(xsem)
            P.dma("sync", lambda sy, h=h, hp=hp: sy.dma_start(
                out=vTb[:, hp, :],
                in_=qkv_d[(32 + h) * 128:(33 + h) * 128, :]),
                xsem, wait=[("c_v", vmul_done.get(h - 2, 0))])
            load_info[h] = (xsem, qk_tgt, P.val(xsem))

        def emit_colsum_piece(nk8=2):
            if state["pending"] is None:
                return
            item, slots, k8_0, rbc_tgt = state["pending"]
            h, gi, qis = item
            first = (gi == 0)
            k8_end = min(k8_0 + nk8, 8)
            for k8 in range(k8_0, k8_end):
                pb = 6 + (k8 % 2)
                w = BK.wait_for([pb])
                if k8 == 0:
                    w = w + [("c_v", rbc_tgt)]
                P.wait("tensor", w)
                nsl = len(slots)
                for j, slot in enumerate(slots):
                    P.op("tensor", lambda t, slot=slot, k8=k8, pb=pb,
                         j=j, nsl=nsl: t.matmul(
                             pall[:, pb, :], rbc[:, slot, :],
                             Ebuf[:, slot, k8 * 512:(k8 + 1) * 512],
                             start=(j == 0), stop=(j == nsl - 1),
                             skip_group_check=True),
                         inc=[("c_t", 1)] if j == nsl - 1 else ())
                tcs = P.val("c_t")
                if first:
                    P.op("vector", lambda v, k8=k8, pb=pb: v.tensor_copy(
                        out=cacc[:, k8 * 512:(k8 + 1) * 512],
                        in_=pall[:, pb, :]),
                        wait=[("c_t", tcs)], chain=(k8 == 0))
                else:
                    P.op("vector", lambda v, k8=k8, pb=pb: v.tensor_tensor(
                        out=cacc[:, k8 * 512:(k8 + 1) * 512],
                        in0=pall[:, pb, :],
                        in1=cacc[:, k8 * 512:(k8 + 1) * 512], op=ALU.add),
                        wait=[("c_t", tcs)], chain=(k8 == 0))
                BK.set([pb], "c_v", P.val("c_v"))
            if k8_end == 8:
                cd = P.val("c_t")
                for slot in slots:
                    slot_free[slot] = cd
                    rbc_free[slot] = cd
                if gi == NGRP - 1:
                    hp = h % 2
                    xsem, _, v_tgt = load_info[h]
                    P.op("vector", lambda v, hp=hp: v.tensor_tensor(
                        out=valsb[:, hp, :], in0=vTb[:, hp, :], in1=cacc[:],
                        op=ALU.mult),
                        wait=[(xsem, v_tgt),
                              ("d_sv", val_store.get(h - 2, 0))])
                    vmul_done[h] = P.val("c_v")
                    P.dma("sync", lambda sy, h=h, hp=hp: sy.dma_start(
                        out=val_d[h * 128:(h + 1) * 128, :],
                        in_=valsb[:, hp, :]),
                        "d_sv", wait=[("c_v", vmul_done[h])])
                    val_store[h] = P.val("d_sv")
                    scores_done[h] = P.val("c_t")
                state["pending"] = None
            else:
                state["pending"] = (item, slots, k8_end, rbc_tgt)

        def emit_scores(item):
            h, gi, qis = item
            hp = h % 2
            if gi == 0:
                emit_loads(h)
            xsem, qk_tgt, _ = load_info[h]
            qT = qkT[:, hp, 0, :]
            kT = qkT[:, hp, 1, :]
            slots = [((h * NGRP + gi) % 3) * GRP + j for j in range(len(qis))]
            for slot, qi in zip(slots, qis):
                for kq in range(4):
                    b = state["nq"] % 3
                    state["nq"] += 1
                    P.wait("tensor", BK.wait_for([2 * b, 2 * b + 1])
                           + [(xsem, qk_tgt)])
                    for half in range(2):
                        P.op("tensor", lambda t, qi=qi, kq=kq, b=b,
                             half=half, qT=qT, kT=kT: t.matmul(
                                 pall[:, 2 * b + half, :],
                                 qT[:, qi * 128:(qi + 1) * 128],
                                 kT[:, kq * 1024 + half * 512:
                                    kq * 1024 + (half + 1) * 512],
                                 start=True, stop=True,
                                 skip_group_check=True),
                             inc=[("c_t", 1)] if half == 1 else ())
                    tsc = P.val("c_t")
                    ew = [("c_t", max(tsc, slot_free.get(slot, 0)))]
                    P.op("scalar", lambda s, slot=slot, kq=kq, b=b:
                         s.activation(
                             out=Ebuf[:, slot, kq * 1024:(kq + 1) * 1024],
                             in_=pall[:, 2 * b:2 * b + 2, :], func=AF.Exp,
                             scale=QSCALE,
                             accum_out=lparts[:, slot * 4 + kq:
                                              slot * 4 + kq + 1]),
                         wait=ew + [("c_v", lp_free.get(slot, 0))],
                         chain=False)
                    BK.set([2 * b, 2 * b + 1], "c_s", P.val("c_s"))
                P.op("vector", lambda v, slot=slot: v.tensor_reduce(
                    out=lsum[:, slot:slot + 1],
                    in_=lparts[:, slot * 4:slot * 4 + 4],
                    axis=AX.X, op=ALU.add),
                    wait=[("c_s", P.val("c_s"))], chain=False)
                lp_free[slot] = P.val("c_v")
                P.op("vector", lambda v, slot=slot: v.reciprocal(
                    out=lr[:, slot:slot + 1], in_=lsum[:, slot:slot + 1]))
                P.op("vector", lambda v, slot=slot: v.tensor_scalar(
                    out=rbc[:, slot, :], in0=ones_bf[:],
                    scalar1=lr[:, slot:slot + 1],
                    scalar2=None, op0=ALU.mult),
                    wait=[("c_t", rbc_free.get(slot, 0))])
                emit_colsum_piece(2)
            return slots

        for item in items:
            slots = emit_scores(item)
            assert state["pending"] is None
            state["pending"] = (item, slots, 0, P.val("c_v"))
        while state["pending"] is not None:
            emit_colsum_piece(2)
        P.flush_block()


def _phase3_outproj(nc, P, pall, wo_d, val_d, attn_d, x_d, r1_d, h2_d, eps_t):
    with nc.sbuf_tensor("wo_sb", [128, NCH, D], BF16) as wo_sb, \
         nc.sbuf_tensor("vslab", [128, 2, NCH, 512], BF16) as vslab, \
         nc.sbuf_tensor("attn_sb", [128, 1, NCH, 512], BF16) as attn_sb, \
         nc.sbuf_tensor("attnT", [128, 2, 4, D], BF16) as attnT, \
         nc.sbuf_tensor("p3xt2", [128, 2, D], F32) as xt2, \
         nc.sbuf_tensor("p3y1sb", [128, D], F32) as y1sb, \
         nc.sbuf_tensor("p3r1sb", [128, 2, D], F32) as r1sb, \
         nc.sbuf_tensor("r1ld", [128, D], F32) as r1ld, \
         nc.sbuf_tensor("p3h2sb", [128, 2, D], BF16) as h2sb, \
         nc.sbuf_tensor("p3bn2", [128, 1, 4, 6], F32) as bn2, \
         nc.sbuf_tensor("p3mvb1", [128, 2, 4, 2], F32) as mvb1, \
         nc.sbuf_tensor("p3mvb2", [128, 2, 4, 2], F32) as mvb2, \
         nc.sbuf_tensor("p3sqb1", [128, 4], F32) as sqb1, \
         nc.sbuf_tensor("p3rstd1", [128, 4], F32) as rstd1, \
         nc.sbuf_tensor("p3nmb1", [128, 4], F32) as nmb1, \
         nc.sbuf_tensor("p3sqb2", [128, 4], F32) as sqb2, \
         nc.sbuf_tensor("p3rstd2", [128, 4], F32) as rstd2, \
         nc.sbuf_tensor("p3nmb2", [128, 4], F32) as nmb2:

        for vf in range(NCH):
            P.dma("gpsimd", lambda g, vf=vf: g.dma_start(
                out=wo_sb[:, vf, :], in_=wo_d[vf * 128:(vf + 1) * 128, :]),
                "d_wa", batch=(vf > 0))
        wo_tgt = P.val("d_wa")

        BK = Banks()
        vload = {}
        attn_store = {}
        stats1 = {}
        stats2 = {}
        mid_v = {}
        xt2_free = {0: 0, 1: 0}
        r1st_free = {0: 0, 1: 0}
        r1ld_free = {0: 0, 1: 0}
        h2st_free = {0: 0, 1: 0}
        r1st_done = {}

        def fold_front(tc):
            st = tc % 2
            psem = "d_sa" if tc % 2 == 0 else "d_sb"
            for j in range(4):
                tt = tc * 4 + j
                w = ([(psem, attn_store[tc]), ("c_v", mid_v.get(tc - 2, 0))]
                     if j == 0 else [])
                P.dma("scalar", lambda s, tt=tt, st=st, j=j: s.dma_start(
                    out=attnT[:, st, j, :],
                    in_=attn_d[:, tt * 128:(tt + 1) * 128], transpose=True),
                    "d_tr", wait=w, batch=(j > 0))
            tr = P.val("d_tr")
            P.wait("vector", [("d_tr", tr)])
            for j in range(4):
                for q in range(4):
                    P.op("vector", lambda v, st=st, j=j, q=q: v.bn_stats(
                        out=bn2[:, 0, q, :],
                        in_=attnT[:, st, j, q * 512:(q + 1) * 512]),
                        chain=(q == 0))
                P.op("vector", lambda v, st=st, j=j: v.bn_aggr(
                    out=mvb1[:, st, j, :], in_=bn2[:, 0, :, :]))
            stats1[tc] = P.val("c_v")

        def fold_mid(tc):
            st = tc % 2
            P.op("scalar", lambda s, st=st: s.activation(
                out=sqb1[:], in_=mvb1[:, st, :, 1], func=AF.Sqrt,
                bias=eps_t[:], scale=1.0), wait=[("c_v", stats1[tc])])
            sq = P.val("c_s")
            P.op("vector", lambda v: v.reciprocal(out=rstd1[:], in_=sqb1[:]),
                 wait=[("c_s", sq)])
            P.op("vector", lambda v, st=st: v.scalar_tensor_tensor(
                out=nmb1[:], in0=mvb1[:, st, :, 0], scalar=-1.0, in1=rstd1[:],
                op0=ALU.mult, op1=ALU.mult))
            for j in range(4):
                tt = tc * 4 + j
                jp = j % 2
                P.dma("gpsimd", lambda g, tt=tt, jp=jp: g.dma_start(
                    out=xt2[:, jp, :], in_=x_d[tt * 128:(tt + 1) * 128, :]),
                    "d_xa", wait=[("c_v", xt2_free[jp])])
                x_t = P.val("d_xa")
                P.op("vector", lambda v, jp=jp, j=j: v.tensor_scalar(
                    out=y1sb[:], in0=xt2[:, jp, :], scalar1=nmb1[:, j:j + 1],
                    scalar2=None, op0=ALU.add), wait=[("d_xa", x_t)])
                xt2_free[jp] = P.val("c_v")
                P.op("vector", lambda v, st=st, j=j, jp=jp:
                     v.scalar_tensor_tensor(
                         out=r1sb[:, jp, :], in0=attnT[:, st, j, :],
                         scalar=rstd1[:, j:j + 1], in1=y1sb[:],
                         op0=ALU.mult, op1=ALU.add),
                     wait=[("d_wv", r1st_free[jp])])
                P.dma("sync", lambda sy, tt=tt, jp=jp: sy.dma_start(
                    out=r1_d[tt * 128:(tt + 1) * 128, :], in_=r1sb[:, jp, :]),
                    "d_wv", wait=[("c_v", P.val("c_v"))])
                r1st_free[jp] = P.val("d_wv")
                for q in range(4):
                    P.op("vector", lambda v, jp=jp, q=q: v.bn_stats(
                        out=bn2[:, 0, q, :],
                        in_=r1sb[:, jp, q * 512:(q + 1) * 512]),
                        chain=(q == 0))
                P.op("vector", lambda v, st=st, j=j: v.bn_aggr(
                    out=mvb2[:, st, j, :], in_=bn2[:, 0, :, :]))
            mid_v[tc] = P.val("c_v")
            stats2[tc] = P.val("c_v")
            r1st_done[tc] = P.val("d_wv")

        def fold_tail(tc):
            st = tc % 2
            P.op("scalar", lambda s, st=st: s.activation(
                out=sqb2[:], in_=mvb2[:, st, :, 1], func=AF.Sqrt,
                bias=eps_t[:], scale=1.0), wait=[("c_v", stats2[tc])])
            sq = P.val("c_s")
            P.op("vector", lambda v: v.reciprocal(out=rstd2[:], in_=sqb2[:]),
                 wait=[("c_s", sq)])
            P.op("vector", lambda v, st=st: v.scalar_tensor_tensor(
                out=nmb2[:], in0=mvb2[:, st, :, 0], scalar=-1.0, in1=rstd2[:],
                op0=ALU.mult, op1=ALU.mult))
            for j in range(4):
                tt = tc * 4 + j
                jp = j % 2
                w = [("c_v", r1ld_free[0])]
                if j == 0:
                    w.append(("d_wv", r1st_done[tc]))
                P.dma("gpsimd", lambda g, tt=tt: g.dma_start(
                    out=r1ld[:], in_=r1_d[tt * 128:(tt + 1) * 128, :]),
                    "d_xb", wait=w)
                rt = P.val("d_xb")
                P.op("vector", lambda v, jp=jp, j=j: v.tensor_scalar(
                    out=h2sb[:, jp, :], in0=r1ld[:],
                    scalar1=rstd2[:, j:j + 1], scalar2=nmb2[:, j:j + 1],
                    op0=ALU.mult, op1=ALU.add),
                    wait=[("d_xb", rt), ("d_sg", h2st_free[jp])])
                r1ld_free[0] = P.val("c_v")
                P.dma("sync", lambda sy, tt=tt, jp=jp: sy.dma_start(
                    out=h2_d[tt * 128:(tt + 1) * 128, :], in_=h2sb[:, jp, :]),
                    "d_sg", wait=[("c_v", P.val("c_v"))])
                h2st_free[jp] = P.val("d_sg")

        for tc in range(NTC):
            P.dma("sync", lambda sy, tc=tc: sy.dma_start(
                out=vslab[:, tc % 2, 0:8, :],
                in_=val_d[0:1024, tc * 512:(tc + 1) * 512]
                .rearrange("(v p) t -> p v t", p=128)),
                "d_sv", wait=[("c_t", vload.get(tc - 2, 0))], batch=(tc > 0))
            P.dma("gpsimd", lambda sc, tc=tc: sc.dma_start(
                out=vslab[:, tc % 2, 8:16, :],
                in_=val_d[1024:2048, tc * 512:(tc + 1) * 512]
                .rearrange("(v p) t -> p v t", p=128)),
                "d_sv", batch=True, wait=[("c_t", vload.get(tc - 2, 0))])
            vl_tgt = P.val("d_sv")
            ssem2 = "d_sa" if tc % 2 == 0 else "d_sb"
            for fo in range(NCH):
                bank = fo % 4
                tw = [("d_sv", vl_tgt)] + BK.wait_for([bank])
                if tc == 0 and fo == 0:
                    tw.append(("d_wa", wo_tgt))
                P.wait("tensor", tw)
                for vf in range(NCH):
                    P.op("tensor", lambda t, vf=vf, fo=fo, tc=tc, bank=bank:
                         t.matmul(
                             pall[:, bank, :],
                             wo_sb[:, vf, fo * 128:(fo + 1) * 128],
                             vslab[:, tc % 2, vf, :],
                             start=(vf == 0), stop=(vf == NCH - 1),
                             skip_group_check=True),
                         inc=[("c_t", 1)] if vf == NCH - 1 else ())
                tpo = P.val("c_t")
                if fo == NCH - 1:
                    vload[tc] = tpo
                P.op("scalar", lambda s, fo=fo, bank=bank: s.activation(
                    out=attn_sb[:, 0, fo, :], in_=pall[:, bank, :],
                    func=AF.Copy),
                    wait=[("c_t", tpo),
                          (("d_sb" if tc % 2 == 0 else "d_sa"),
                           attn_store.get(tc - 1, 0))],
                    chain=False)
                BK.set([bank], "c_s", P.val("c_s"))
            attn_evac = P.val("c_s")
            P.dma("sync", lambda sy, tc=tc: sy.dma_start(
                out=attn_d[0:1024, tc * 512:(tc + 1) * 512]
                .rearrange("(f p) t -> p f t", p=128),
                in_=attn_sb[:, 0, 0:8, :]),
                ssem2, wait=[("c_s", attn_evac)])
            P.dma("gpsimd", lambda sc, tc=tc: sc.dma_start(
                out=attn_d[1024:2048, tc * 512:(tc + 1) * 512]
                .rearrange("(f p) t -> p f t", p=128),
                in_=attn_sb[:, 0, 8:16, :]),
                ssem2, batch=True, wait=[("c_s", attn_evac)])
            attn_store[tc] = P.val(ssem2)
            if tc >= 1:
                fold_mid(tc - 1)
            if tc >= 2:
                fold_tail(tc - 2)
            fold_front(tc)
        fold_mid(NTC - 1)
        fold_tail(NTC - 2)
        fold_tail(NTC - 1)
        P.flush_block()


def _phase4_ffn(nc, P, pall, sgt, sut, sdt, wg_d, wu_d, wd_d, h2_d, down_d,
                r1_d, out_d, eps_t):
    with nc.sbuf_tensor("h2T", [128, 2, NCH, 512], BF16) as h2T, \
         nc.sbuf_tensor("wg_sb", [128, 3, NCH, 256], BF16) as wg_sb, \
         nc.sbuf_tensor("wu_sb", [128, 3, NCH, 256], BF16) as wu_sb, \
         nc.sbuf_tensor("wd_sb", [128, 2, 2, D], BF16) as wd_sb, \
         nc.sbuf_tensor("fuse", [128, NI, 512], BF16) as fuse, \
         nc.sbuf_tensor("sgl", [128, 2, 512], BF16) as sgl, \
         nc.sbuf_tensor("down_sb", [128, 2, 2, 512], BF16) as down_sb, \
         nc.sbuf_tensor("downT", [128, 4, D], BF16) as downT, \
         nc.sbuf_tensor("r1t", [128, D], F32) as r1t, \
         nc.sbuf_tensor("y2sb", [128, D], F32) as y2sb, \
         nc.sbuf_tensor("outsb", [128, D], F32) as outsb, \
         nc.sbuf_tensor("bn3", [128, 1, 4, 6], F32) as bn3, \
         nc.sbuf_tensor("mvb3", [128, 4, 2], F32) as mvb3, \
         nc.sbuf_tensor("sqb3", [128, 4], F32) as sqb3, \
         nc.sbuf_tensor("rstdb3", [128, 4], F32) as rstdb3, \
         nc.sbuf_tensor("nmbb3", [128, 4], F32) as nmbb3:

        NHG = 32
        BK = Banks()
        gu_done = {}
        wgu_free = {0: 0, 1: 0, 2: 0}
        wd_free = {0: 0, 1: 0}
        sgl_free = {0: 0, 1: 0}
        fuse_done = {}
        down_read = {}
        down_store_tc = {}
        down_store2 = {}

        preloaded = {}
        gctr = {"n": 0}
        epi = {"apply_v": 0, "stats_v": 0, "st": 0, "tr": 0}

        def load_wgu(hg):
            gb = gctr["n"] % 3
            gctr["n"] += 1
            P.dma("gpsimd", lambda gg, hg=hg, gb=gb: gg.dma_start(
                out=wg_sb[:, gb, :, :],
                in_=wg_d[:, hg * 256:(hg + 1) * 256]
                .rearrange("(c p) i -> p c i", p=128)),
                "d_wa", wait=[("c_t", wgu_free[gb])])
            P.dma("gpsimd", lambda gg, hg=hg, gb=gb: gg.dma_start(
                out=wu_sb[:, gb, :, :],
                in_=wu_d[:, hg * 256:(hg + 1) * 256]
                .rearrange("(c p) i -> p c i", p=128)), "d_wa", batch=True)
            return "d_wa", P.val("d_wa"), gb

        def emit_epilogue_front(tc):
            for j in range(4):
                tt = tc * 4 + j
                w = ([("d_sg", down_store_tc[tc]),
                      ("c_v", epi["apply_v"])] if j == 0 else [])
                P.dma("scalar", lambda s, tt=tt, j=j: s.dma_start(
                    out=downT[:, j, :],
                    in_=down_d[:, tt * 128:(tt + 1) * 128],
                    transpose=True), "d_xa", wait=w, batch=(j > 0))
            epi["tr"] = P.val("d_xa")
            P.wait("vector", [("d_xa", epi["tr"])])
            for j in range(4):
                for q in range(4):
                    P.op("vector", lambda v, j=j, q=q: v.bn_stats(
                        out=bn3[:, 0, q, :],
                        in_=downT[:, j, q * 512:(q + 1) * 512]),
                        chain=(j == 0 and q == 0))
                P.op("vector", lambda v, j=j: v.bn_aggr(
                    out=mvb3[:, j, :], in_=bn3[:, 0, :, :]))
            epi["stats_v"] = P.val("c_v")

        def emit_epilogue_back(tc):
            P.op("scalar", lambda s: s.activation(
                out=sqb3[:], in_=mvb3[:, :, 1], func=AF.Sqrt, bias=eps_t[:],
                scale=1.0), wait=[("c_v", epi["stats_v"])])
            sq_s = P.val("c_s")
            P.op("vector", lambda v: v.reciprocal(out=rstdb3[:], in_=sqb3[:]),
                 wait=[("c_s", sq_s)])
            P.op("vector", lambda v: v.scalar_tensor_tensor(
                out=nmbb3[:], in0=mvb3[:, :, 0], scalar=-1.0, in1=rstdb3[:],
                op0=ALU.mult, op1=ALU.mult))
            for j in range(4):
                tt = tc * 4 + j
                P.dma("sync", lambda sy, tt=tt: sy.dma_start(
                    out=r1t[:], in_=r1_d[tt * 128:(tt + 1) * 128, :]),
                    "d_xb", wait=[("c_v", epi["apply_v"])])
                r1_tgt = P.val("d_xb")
                P.op("vector", lambda v, j=j: v.tensor_scalar(
                    out=y2sb[:], in0=r1t[:], scalar1=nmbb3[:, j:j + 1],
                    scalar2=None, op0=ALU.add),
                    wait=[("d_xb", r1_tgt), ("d_sa", epi["st"])])
                P.op("vector", lambda v, j=j: v.scalar_tensor_tensor(
                    out=outsb[:], in0=downT[:, j, :],
                    scalar=rstdb3[:, j:j + 1], in1=y2sb[:],
                    op0=ALU.mult, op1=ALU.add))
                epi["apply_v"] = P.val("c_v")
                P.dma("sync", lambda sy, tt=tt: sy.dma_start(
                    out=out_d[tt * 128:(tt + 1) * 128, :], in_=outsb[:]),
                    "d_sa", wait=[("c_v", epi["apply_v"])])
                epi["st"] = P.val("d_sa")

        def emit_h2T(tc):
            tb = tc % 2
            for c in range(NCH):
                tr_w = ([("d_tr", P.val("d_tr")),
                         ("c_t", gu_done.get(tc - 2, 0))] if c == 0 else [])
                P.dma("sync", lambda s, tc=tc, tb=tb, c=c: s.dma_start(
                    out=h2T[:, tb, c, :],
                    in_=h2_d[tc * 512:(tc + 1) * 512, c * 128:(c + 1) * 128],
                    transpose=True),
                    "d_tr", wait=tr_w, batch=(c > 0))
            return P.val("d_tr")

        h2T_tgts = {}
        for tc in range(NTC):
            tb = tc % 2
            if tc not in h2T_tgts:
                h2T_tgts[tc] = emit_h2T(tc)
            h2T_tgt = h2T_tgts[tc]
            if tc >= 1:
                emit_epilogue_front(tc - 1)

            for hg in range(NHG):
                for la in (hg, hg + 1, hg + 2):
                    if la < NHG and (tc, la) not in preloaded:
                        preloaded[(tc, la)] = load_wgu(la)
                gsem, wg_tgt, gb = preloaded.pop((tc, hg))
                for i2 in range(2):
                    i = hg * 2 + i2
                    pg = (0, 1, 4)[i % 3]
                    pu = (2, 3, 5)[i % 3]
                    P.wait("tensor", [(gsem, wg_tgt), ("d_tr", h2T_tgt)]
                           + BK.wait_for([pg, pu]))
                    for c in range(NCH):
                        P.op("tensor", lambda t, c=c, i2=i2, gb=gb, tb=tb,
                             pg=pg: t.matmul(
                                 pall[:, pg, :],
                                 wg_sb[:, gb, c, i2 * 128:(i2 + 1) * 128],
                                 h2T[:, tb, c, :],
                                 start=(c == 0), stop=(c == NCH - 1),
                                 skip_group_check=True),
                             inc=[("c_t", 1)] if c == NCH - 1 else ())
                    tg = P.val("c_t")
                    for c in range(NCH):
                        P.op("tensor", lambda t, c=c, i2=i2, gb=gb, tb=tb,
                             pu=pu: t.matmul(
                                 pall[:, pu, :],
                                 wu_sb[:, gb, c, i2 * 128:(i2 + 1) * 128],
                                 h2T[:, tb, c, :],
                                 start=(c == 0), stop=(c == NCH - 1),
                                 skip_group_check=True),
                             inc=[("c_t", 1)] if c == NCH - 1 else ())
                    tu = P.val("c_t")
                    P.op("scalar", lambda s, i=i, pg=pg: s.activation(
                        out=sgl[:, i % 2, :], in_=pall[:, pg, :],
                        func=AF.Silu, scale=sgt[:, i:i + 1]),
                        wait=[("c_t", tg), ("c_v", sgl_free[i % 2])],
                        chain=False)
                    BK.set([pg], "c_s", P.val("c_s"))
                    P.op("vector", lambda v, i=i, pu=pu: v.scalar_tensor_tensor(
                        out=fuse[:, i, :], in0=pall[:, pu, :],
                        scalar=sut[:, i:i + 1], in1=sgl[:, i % 2, :],
                        op0=ALU.mult, op1=ALU.mult),
                        wait=[("c_t", tu), ("c_s", P.val("c_s")),
                              ("c_t", down_read.get(tc - 1, 0))],
                        chain=False)
                    BK.set([pu], "c_v", P.val("c_v"))
                    sgl_free[i % 2] = P.val("c_v")
                wgu_free[gb] = P.val("c_t")
            gu_done[tc] = P.val("c_t")
            fuse_done[tc] = P.val("c_v")
            if tc >= 1:
                emit_epilogue_back(tc - 1)
            if tc + 1 < NTC:
                for hg in range(2):
                    preloaded[(tc + 1, hg)] = load_wgu(hg)

            for p in range(2):
                if p == 1 and tc + 1 < NTC:
                    h2T_tgts[tc + 1] = emit_h2T(tc + 1)
                for s2 in range(32):
                    wdb = s2 % 2
                    dsem = "d_wb" if wdb == 0 else "d_wv"
                    P.dma("gpsimd", lambda gg, s2=s2, wdb=wdb: gg.dma_start(
                        out=wd_sb[:, wdb, :, :],
                        in_=wd_d[s2 * 256:(s2 + 1) * 256, :]
                        .rearrange("(q p) f -> p q f", p=128)),
                        dsem, wait=[("c_t", wd_free[wdb])])
                    wd_tgt = P.val(dsem)
                    tw = [(dsem, wd_tgt), ("c_v", fuse_done[tc])]
                    if s2 == 0:
                        tw += BK.wait_for(list(range(8)))
                        tw.append(("c_s", P.val("c_s")))
                    P.wait("tensor", tw)
                    for q in range(2):
                        i = s2 * 2 + q
                        for f8 in range(8):
                            fo = p * 8 + f8
                            P.op("tensor", lambda t, q=q, f8=f8, fo=fo, i=i,
                                 wdb=wdb: t.matmul(
                                     pall[:, f8, :],
                                     wd_sb[:, wdb, q, fo * 128:(fo + 1) * 128],
                                     fuse[:, i, :],
                                     start=(i == 0), stop=(i == NI - 1),
                                     skip_group_check=True),
                                 inc=[("c_t", 1)]
                                 if (q == 1 and f8 == 7) else ())
                    wd_free[wdb] = P.val("c_t")
                tdn = P.val("c_t")
                for grp in range(4):
                    hb = grp % 2
                    for f2 in range(2):
                        f8 = grp * 2 + f2
                        fo = p * 8 + f8
                        P.op("scalar", lambda s, f2=f2, f8=f8, fo=fo,
                             hb=hb: s.activation(
                                 out=down_sb[:, hb, f2, :],
                                 in_=pall[:, f8, :],
                                 func=AF.Copy, scale=sdt[:, fo:fo + 1]),
                             wait=[("c_t", tdn),
                                   ("d_sg", down_store2.get(hb, 0))],
                             chain=False)
                    BK.set([grp * 2, grp * 2 + 1], "c_s", P.val("c_s"))
                    P.dma("sync", lambda sy, tc=tc, p=p, grp=grp, hb=hb:
                          sy.dma_start(
                              out=down_d[(p * 8 + grp * 2) * 128:
                                         (p * 8 + grp * 2 + 2) * 128,
                                         tc * 512:(tc + 1) * 512]
                              .rearrange("(f q) t -> q f t", q=128),
                              in_=down_sb[:, hb, :, :]),
                          "d_sg", wait=[("c_s", P.val("c_s"))])
                    down_store2[hb] = P.val("d_sg")
            down_store_tc[tc] = P.val("d_sg")
            down_read[tc] = P.val("c_t")
        emit_epilogue_front(NTC - 1)
        emit_epilogue_back(NTC - 1)
        P.flush_block()


def kernel(L0_attn_x, L0_attn_w_qkv, L0_attn_w_o, L0_ffn_w_gate, L0_ffn_w_up,
           L0_ffn_w_down, L0_ffn_s_gate, L0_ffn_s_up, L0_ffn_s_down):
    nc = build()
    x = np.ascontiguousarray(np.asarray(L0_attn_x, dtype=np.float32))
    shared = {
        "wqkv": np.ascontiguousarray(np.asarray(L0_attn_w_qkv, dtype=np.float32)),
        "wo": np.ascontiguousarray(np.asarray(L0_attn_w_o, dtype=np.float32)),
        "wg": np.ascontiguousarray(np.asarray(L0_ffn_w_gate, dtype=np.int8)),
        "wu": np.ascontiguousarray(np.asarray(L0_ffn_w_up, dtype=np.int8)),
        "wd": np.ascontiguousarray(np.asarray(L0_ffn_w_down, dtype=np.int8)),
        "sg": np.ascontiguousarray(np.asarray(L0_ffn_s_gate, dtype=np.float32).reshape(NI, 128).T),
        "su": np.ascontiguousarray(np.asarray(L0_ffn_s_up, dtype=np.float32).reshape(NI, 128).T),
        "sd": np.ascontiguousarray(np.asarray(L0_ffn_s_down, dtype=np.float32).reshape(NCH, 128).T),
    }
    in_maps = [dict(shared, x=np.ascontiguousarray(x[b])) for b in range(8)]
    res = run_bass_kernel_spmd(nc, in_maps, core_ids=list(range(8)))
    return np.stack([res.results[b]["out"] for b in range(8)], axis=0)


# version 24
# speedup vs baseline: 1.2317x; 1.0320x over previous
import numpy as np
from contextlib import ExitStack

import concourse.bass as bass
import concourse.mybir as mybir
from concourse.bass_utils import run_bass_kernel_spmd

F32 = mybir.dt.float32
BF16 = mybir.dt.bfloat16
I8 = mybir.dt.int8
AF = mybir.ActivationFunctionType
ALU = mybir.AluOpType
AX = mybir.AxisListType

S_TOK = 4096
D = 2048
H = 16
INTER = 8192
NT = S_TOK // 128
NCH = D // 128
NI = INTER // 128
NTC = 8
EPS = 1e-5
QSCALE = float(1.0 / np.sqrt(128.0))
GRP = 4

ENGS = ("tensor", "scalar", "vector", "sync", "gpsimd")
SEM_NAMES = ("d_xa", "d_xb", "d_sa", "d_sb", "d_sv", "d_sg", "d_wa", "d_wb",
             "d_wv", "d_tr", "c_t", "c_s", "c_v", "c_g")


class Prog:
    def __init__(self, nc):
        self.nc = nc
        self.ops = {e: [] for e in ENGS}
        self.cnt = {s: 0 for s in SEM_NAMES}
        self.stack = ExitStack()
        self.S = {s: self.stack.enter_context(nc.semaphore(s)) for s in SEM_NAMES}

    def _emit(self, eng, fn, waits, incs):
        S = self.S
        waits = tuple((s, t) for s, t in waits if t > 0)
        incs = tuple(incs)

        def run(e):
            for s, t in waits:
                e.wait_ge(S[s], t)
            ins = fn(e)
            for s, n in incs:
                ins.then_inc(S[s], n)
            return ins

        self.ops[eng].append(run)

    def op(self, eng, fn, wait=(), inc=(), chain=True):
        waits = list(wait)
        incs = list(inc)
        if eng == "scalar":
            assert not incs
            if chain:
                waits.append(("c_s", self.cnt["c_s"]))
            self.cnt["c_s"] += 1
            incs.append(("c_s", 1))
        elif eng == "vector":
            assert not incs
            if chain:
                waits.append(("c_v", self.cnt["c_v"]))
            self.cnt["c_v"] += 1
            incs.append(("c_v", 1))
        else:
            for s, n in incs:
                self.cnt[s] += n
        self._emit(eng, fn, waits, incs)

    def dma(self, eng, fn, sem, wait=(), batch=False):
        waits = list(wait)
        if not batch:
            waits.append((sem, self.cnt[sem]))
        self.cnt[sem] += 16
        self._emit(eng, fn, waits, [(sem, 16)])

    def wait(self, eng, wait):
        S = self.S
        waits = tuple((s, t) for s, t in wait if t > 0)
        if not waits:
            return

        def run(e):
            for s, t in waits:
                e.wait_ge(S[s], t)

        self.ops[eng].append(run)

    def val(self, s):
        return self.cnt[s]

    def phase_barrier(self):
        w = [(s, self.cnt[s]) for s in SEM_NAMES]
        for e in ENGS:
            self.wait(e, w)

    def flush_block(self):
        ops = self.ops
        with self.nc.Block() as b:
            @b.tensor
            def _(t):
                for f in ops["tensor"]:
                    f(t)

            @b.scalar
            def _(s):
                for f in ops["scalar"]:
                    f(s)

            @b.vector
            def _(v):
                for f in ops["vector"]:
                    f(v)

            @b.sync
            def _(sy):
                for f in ops["sync"]:
                    f(sy)

            @b.gpsimd
            def _(g):
                for f in ops["gpsimd"]:
                    f(g)
        self.ops = {e: [] for e in ENGS}



def _batched_ln(P, nb, tiles_ap, bnbuf, mvb, sqb, rstdb, nmbb, eps_t):
    for j in range(nb):
        for q in range(4):
            P.op("vector", lambda v, j=j, q=q, t=tiles_ap: v.bn_stats(
                out=bnbuf[:, j, q, :], in_=t(j)[:, q * 512:(q + 1) * 512]),
                chain=(j == 0 and q == 0))
    for j in range(nb):
        P.op("vector", lambda v, j=j: v.bn_aggr(
            out=mvb[:, j, :], in_=bnbuf[:, j, :, :]), chain=(j == 0))
    P.op("scalar", lambda s: s.activation(
        out=sqb[:], in_=mvb[:, :, 1], func=AF.Sqrt, bias=eps_t[:], scale=1.0),
        wait=[("c_v", P.val("c_v"))])
    P.op("vector", lambda v: v.reciprocal(out=rstdb[:], in_=sqb[:]),
         wait=[("c_s", P.val("c_s"))])
    P.op("vector", lambda v: v.scalar_tensor_tensor(
        out=nmbb[:], in0=mvb[:, :, 0], scalar=-1.0, in1=rstdb[:],
        op0=ALU.mult, op1=ALU.mult))


def _ln_stats(P, tile_ap, bnbuf, mv):
    for j in range(4):
        P.op("vector", lambda v, t=tile_ap, b=bnbuf, j=j: v.bn_stats(
            out=b[:, j, :], in_=t[:, j * 512:(j + 1) * 512]))
    P.op("vector", lambda v, b=bnbuf, m=mv: v.bn_aggr(out=m[:], in_=b[:]))


def _ln_scalebias(P, mv, sq, rstd, nmb, eps_t):
    P.op("scalar", lambda s, m=mv, q=sq, e=eps_t: s.activation(
        out=q[:], in_=m[:, 1:2], func=AF.Sqrt, bias=e[:], scale=1.0),
        wait=[("c_v", P.val("c_v"))])
    P.op("vector", lambda v, q=sq, r=rstd: v.reciprocal(out=r[:], in_=q[:]),
         wait=[("c_s", P.val("c_s"))])
    P.op("vector", lambda v, m=mv, r=rstd, o=nmb: v.tensor_scalar(
        out=o[:], in0=m[:, 0:1], scalar1=r[:], scalar2=-1.0,
        op0=ALU.mult, op1=ALU.mult))


class Banks:

    def __init__(self):
        self.free = {b: ("c_s", 0) for b in range(8)}

    def wait_for(self, banks):
        return [self.free[b] for b in banks]

    def set(self, banks, sem, tgt):
        for b in banks:
            self.free[b] = (sem, tgt)


def build(debug=0, max_phase=5):
    nc = bass.Bass("TRN2", target_bir_lowering=False)

    x_d = nc.dram_tensor("x", [S_TOK, D], F32, kind="ExternalInput")
    wqkv_d = nc.dram_tensor("wqkv", [D, 3 * D], F32, kind="ExternalInput")
    wo_d = nc.dram_tensor("wo", [D, D], F32, kind="ExternalInput")
    wg_d = nc.dram_tensor("wg", [128, 32, NCH, 256], I8, kind="ExternalInput")
    wu_d = nc.dram_tensor("wu", [128, 32, NCH, 256], I8, kind="ExternalInput")
    wd_d = nc.dram_tensor("wd", [128, 32, 2, D], I8, kind="ExternalInput")
    sg_d = nc.dram_tensor("sg", [128, NI], F32, kind="ExternalInput")
    su_d = nc.dram_tensor("su", [128, NI], F32, kind="ExternalInput")
    sd_d = nc.dram_tensor("sd", [128, NCH], F32, kind="ExternalInput")
    out_d = nc.dram_tensor("out", [S_TOK, D], F32, kind="ExternalOutput")

    sk = "ExternalOutput" if debug else "Internal"
    h1_d = nc.dram_tensor("h1d", [S_TOK, D], BF16, kind=sk)
    val_d = nc.dram_tensor("vald", [D, S_TOK], BF16, kind=sk)
    attn_d = nc.dram_tensor("attnd", [D, S_TOK], BF16, kind=sk)
    r1_d = nc.dram_tensor("r1d", [S_TOK, D], F32, kind=sk)
    h2_d = nc.dram_tensor("h2d", [S_TOK, D], BF16, kind=sk)
    down_d = nc.dram_tensor("downd", [D, S_TOK], BF16, kind=sk)
    qkv_d = nc.dram_tensor("qkvd", [3 * D, S_TOK], BF16, kind=sk)

    es = ExitStack()
    P = Prog(nc)
    es.enter_context(P.stack)

    sgt = es.enter_context(nc.sbuf_tensor("sgt", [128, NI], F32))
    sut = es.enter_context(nc.sbuf_tensor("sut", [128, NI], F32))
    sdt = es.enter_context(nc.sbuf_tensor("sdt", [128, NCH], F32))
    ones_bf = es.enter_context(nc.sbuf_tensor("ones_bf", [128, 128], BF16))
    eps_t = es.enter_context(nc.sbuf_tensor("eps_t", [128, 1], F32))
    pall = es.enter_context(nc.psum_tensor("pall", [128, 8, 512], F32))

    P.dma("gpsimd", lambda g: g.dma_start(
        out=sgt[:], in_=sg_d[:]), "d_wa")
    P.dma("gpsimd", lambda g: g.dma_start(
        out=sut[:], in_=su_d[:]), "d_wa")
    P.dma("gpsimd", lambda g: g.dma_start(
        out=sdt[:], in_=sd_d[:]), "d_wa")
    P.op("gpsimd", lambda g: g.memset(ones_bf[:], 1.0), inc=[("c_g", 1)])
    P.op("gpsimd", lambda g: g.memset(eps_t[:], EPS), inc=[("c_g", 1)])
    for _e in ("scalar", "vector", "tensor"):
        P.wait(_e, [("c_g", 2)])

    h1T_scope = ExitStack()
    h1T = h1T_scope.enter_context(nc.sbuf_tensor("h1T", [128, NCH, S_TOK], BF16))

    NB1 = 4
    with nc.sbuf_tensor("xt", [128, NB1, D], F32) as xt, \
         nc.sbuf_tensor("h1sb", [128, NB1, D], BF16) as h1sb, \
         nc.sbuf_tensor("bnbuf", [128, NB1, 4, 6], F32) as bnbuf, \
         nc.sbuf_tensor("mv1", [128, NB1, 2], F32) as mv1, \
         nc.sbuf_tensor("sq1", [128, NB1], F32) as sq1, \
         nc.sbuf_tensor("rstd1", [128, NB1], F32) as rstd1, \
         nc.sbuf_tensor("nmb1", [128, NB1], F32) as nmb1:

        nbat = NT // NB1
        s_done = {}
        v_done = {}
        st_done = {}
        for bt in range(nbat):
            xsem = "d_xa" if bt % 2 == 0 else "d_xb"
            ssem = "d_sa" if bt % 2 == 0 else "d_sb"
            osem = "d_sb" if bt % 2 == 0 else "d_sa"
            for j in range(NB1):
                tt = bt * NB1 + j
                P.dma("sync", lambda sy, tt=tt, j=j: sy.dma_start(
                    out=xt[:, j, :], in_=x_d[tt * 128:(tt + 1) * 128, :]),
                    xsem, batch=(j > 0),
                    wait=[("c_s", s_done.get(bt - 1, 0))] if j == 0 else [])
            x_tgt = P.val(xsem)
            P.wait("vector", [(xsem, x_tgt), ("c_g", 2)])
            _batched_ln(P, NB1, lambda j: xt[:, j, :], bnbuf, mv1, sq1,
                        rstd1, nmb1, eps_t)
            ln_v = P.val("c_v")
            for j in range(NB1):
                w = [("c_v", ln_v)]
                if j == 0:
                    w.append((osem, st_done.get(bt - 1, 0)))
                P.op("scalar", lambda s, j=j: s.activation(
                    out=h1sb[:, j, :], in_=xt[:, j, :], func=AF.Identity,
                    scale=rstd1[:, j:j + 1], bias=nmb1[:, j:j + 1]),
                    wait=w, chain=(j == 0))
            s_done[bt] = P.val("c_s")
            for j in range(NB1):
                tt = bt * NB1 + j
                P.dma("sync", lambda sy, tt=tt, j=j: sy.dma_start(
                    out=h1_d[tt * 128:(tt + 1) * 128, :], in_=h1sb[:, j, :]),
                    ssem, batch=(j > 0),
                    wait=[("c_s", s_done[bt])] if j == 0 else [])
            st_done[bt] = P.val(ssem)
            v_done[bt] = P.val("c_v")

        st_w = [("d_sa", P.val("d_sa")), ("d_sb", P.val("d_sb"))]
        for c in range(NCH):
            P.dma("sync", lambda s, c=c: s.dma_start(
                out=h1T[:, c, :], in_=h1_d[:, c * 128:(c + 1) * 128],
                transpose=True),
                "d_tr", wait=st_w if c == 0 else [], batch=(c > 0))
        h1T_tgt = P.val("d_tr")
        P.flush_block()

    if max_phase >= 2:
        _phase2_qkv(nc, P, h1T, pall, wqkv_d, qkv_d, h1T_tgt)
        h1T_scope.close()
        P.phase_barrier()
        _phase2b_attn(nc, P, pall, ones_bf, qkv_d, val_d)
    else:
        h1T_scope.close()

    if max_phase >= 3:
        P.phase_barrier()
        _phase3_outproj(nc, P, pall, wo_d, val_d, attn_d, x_d, r1_d, h2_d, eps_t)
    if max_phase >= 4:
        P.phase_barrier()
        _phase4_ffn(nc, P, pall, sgt, sut, sdt, wg_d, wu_d, wd_d, h2_d, down_d,
                    r1_d, out_d, eps_t)

    P.wait("sync", [("d_sa", P.val("d_sa")), ("d_sb", P.val("d_sb")),
                    ("d_sv", P.val("d_sv")), ("d_sg", P.val("d_sg"))])
    P.flush_block()
    es.close()
    return nc


def _phase2_qkv(nc, P, h1T, pall, wqkv_d, qkv_d, h1T_tgt):
    with nc.sbuf_tensor("wqkv_sb", [128, 2, NCH, 128], BF16) as wqkv_sb, \
         nc.sbuf_tensor("qkv_sb", [128, 2, S_TOK], BF16) as qkv_sb:
        BK = Banks()
        t_done = {}
        st_done = {0: 0, 1: 0}
        for f in range(48):
            fb = f % 2
            wsem = "d_wa" if fb == 0 else "d_wb"
            P.dma("gpsimd", lambda g, f=f, fb=fb: g.dma_start(
                out=wqkv_sb[:, fb, :, :],
                in_=wqkv_d[:, f * 128:(f + 1) * 128]
                .rearrange("(c p) x -> p c x", p=128)),
                wsem, wait=[("c_t", t_done.get(f - 2, 0))])
            w_tgt = P.val(wsem)
            ssem = "d_sa" if fb == 0 else "d_sb"
            for ts8 in range(8):
                bank = ts8 % 4
                tw = [(wsem, w_tgt)] + BK.wait_for([bank])
                if f == 0 and ts8 == 0:
                    tw.append(("d_tr", h1T_tgt))
                P.wait("tensor", tw)
                for c in range(NCH):
                    P.op("tensor", lambda t, fb=fb, c=c, ts8=ts8, bank=bank:
                         t.matmul(
                             pall[:, bank, :], wqkv_sb[:, fb, c, :],
                             h1T[:, c, ts8 * 512:(ts8 + 1) * 512],
                             start=(c == 0), stop=(c == NCH - 1),
                             skip_group_check=True),
                         inc=[("c_t", 1)] if c == NCH - 1 else ())
                tmm = P.val("c_t")
                P.op("vector", lambda v, fb=fb, ts8=ts8, bank=bank:
                     v.tensor_copy(
                         out=qkv_sb[:, fb, ts8 * 512:(ts8 + 1) * 512],
                         in_=pall[:, bank, :]),
                     wait=[("c_t", tmm), (ssem, st_done[fb])], chain=False)
                BK.set([bank], "c_v", P.val("c_v"))
            t_done[f] = P.val("c_t")
            P.dma("sync", lambda sy, f=f, fb=fb: sy.dma_start(
                out=qkv_d[f * 128:(f + 1) * 128, :], in_=qkv_sb[:, fb, :]),
                ssem, wait=[("c_v", P.val("c_v"))])
            st_done[fb] = P.val(ssem)
        P.flush_block()


def _phase2b_attn(nc, P, pall, ones_bf, qkv_d, val_d):
    NSLOT = 3 * GRP
    with nc.sbuf_tensor("qkT", [128, 2, 2, S_TOK], BF16) as qkT, \
         nc.sbuf_tensor("vTb", [128, 2, S_TOK], BF16) as vTb, \
         nc.sbuf_tensor("valsb", [128, 2, S_TOK], BF16) as valsb, \
         nc.sbuf_tensor("Ebuf", [128, NSLOT, S_TOK], BF16) as Ebuf, \
         nc.sbuf_tensor("cacc", [128, S_TOK], F32) as cacc, \
         nc.sbuf_tensor("rbc", [128, NSLOT, 128], BF16) as rbc, \
         nc.sbuf_tensor("lparts", [128, NSLOT * 4], F32) as lparts, \
         nc.sbuf_tensor("lsum", [128, NSLOT], F32) as lsum, \
         nc.sbuf_tensor("lr", [128, NSLOT], F32) as lr:

        groups = [list(range(g, g + GRP)) for g in range(0, NT, GRP)]
        NGRP = len(groups)
        items = [(h, gi, groups[gi]) for h in range(H) for gi in range(NGRP)]
        BK = Banks()
        val_store = {}
        scores_done = {}
        vmul_done = {}
        slot_free = {}
        rbc_free = {}
        lp_free = {}
        load_info = {}
        state = {"nq": 0, "pending": None}

        def emit_loads(h):
            hp = h % 2
            xsem = "d_xa" if hp == 0 else "d_xb"
            P.dma("sync", lambda sy, h=h, hp=hp: sy.dma_start(
                out=qkT[:, hp, 0, :], in_=qkv_d[h * 128:(h + 1) * 128, :]),
                xsem, wait=[("c_t", scores_done.get(h - 2, 0))])
            P.dma("sync", lambda sy, h=h, hp=hp: sy.dma_start(
                out=qkT[:, hp, 1, :],
                in_=qkv_d[(16 + h) * 128:(17 + h) * 128, :]),
                xsem, batch=True)
            qk_tgt = P.val(xsem)
            P.dma("sync", lambda sy, h=h, hp=hp: sy.dma_start(
                out=vTb[:, hp, :],
                in_=qkv_d[(32 + h) * 128:(33 + h) * 128, :]),
                xsem, wait=[("c_v", vmul_done.get(h - 2, 0))])
            load_info[h] = (xsem, qk_tgt, P.val(xsem))

        def emit_colsum_piece(nk8=2):
            if state["pending"] is None:
                return
            item, slots, k8_0, rbc_tgt = state["pending"]
            h, gi, qis = item
            first = (gi == 0)
            k8_end = min(k8_0 + nk8, 8)
            for k8 in range(k8_0, k8_end):
                pb = 6 + (k8 % 2)
                w = BK.wait_for([pb])
                if k8 == 0:
                    w = w + [("c_v", rbc_tgt)]
                P.wait("tensor", w)
                nsl = len(slots)
                for j, slot in enumerate(slots):
                    P.op("tensor", lambda t, slot=slot, k8=k8, pb=pb,
                         j=j, nsl=nsl: t.matmul(
                             pall[:, pb, :], rbc[:, slot, :],
                             Ebuf[:, slot, k8 * 512:(k8 + 1) * 512],
                             start=(j == 0), stop=(j == nsl - 1),
                             skip_group_check=True),
                         inc=[("c_t", 1)] if j == nsl - 1 else ())
                tcs = P.val("c_t")
                if first:
                    P.op("vector", lambda v, k8=k8, pb=pb: v.tensor_copy(
                        out=cacc[:, k8 * 512:(k8 + 1) * 512],
                        in_=pall[:, pb, :]),
                        wait=[("c_t", tcs)], chain=(k8 == 0))
                else:
                    P.op("vector", lambda v, k8=k8, pb=pb: v.tensor_tensor(
                        out=cacc[:, k8 * 512:(k8 + 1) * 512],
                        in0=pall[:, pb, :],
                        in1=cacc[:, k8 * 512:(k8 + 1) * 512], op=ALU.add),
                        wait=[("c_t", tcs)], chain=(k8 == 0))
                BK.set([pb], "c_v", P.val("c_v"))
            if k8_end == 8:
                cd = P.val("c_t")
                for slot in slots:
                    slot_free[slot] = cd
                    rbc_free[slot] = cd
                if gi == NGRP - 1:
                    hp = h % 2
                    xsem, _, v_tgt = load_info[h]
                    P.op("vector", lambda v, hp=hp: v.tensor_tensor(
                        out=valsb[:, hp, :], in0=vTb[:, hp, :], in1=cacc[:],
                        op=ALU.mult),
                        wait=[(xsem, v_tgt),
                              ("d_sv", val_store.get(h - 2, 0))])
                    vmul_done[h] = P.val("c_v")
                    P.dma("sync", lambda sy, h=h, hp=hp: sy.dma_start(
                        out=val_d[h * 128:(h + 1) * 128, :],
                        in_=valsb[:, hp, :]),
                        "d_sv", wait=[("c_v", vmul_done[h])])
                    val_store[h] = P.val("d_sv")
                    scores_done[h] = P.val("c_t")
                state["pending"] = None
            else:
                state["pending"] = (item, slots, k8_end, rbc_tgt)

        def emit_scores(item):
            h, gi, qis = item
            hp = h % 2
            if gi == 0:
                emit_loads(h)
            xsem, qk_tgt, _ = load_info[h]
            qT = qkT[:, hp, 0, :]
            kT = qkT[:, hp, 1, :]
            slots = [((h * NGRP + gi) % 3) * GRP + j for j in range(len(qis))]
            for slot, qi in zip(slots, qis):
                for kq in range(4):
                    b = state["nq"] % 3
                    state["nq"] += 1
                    P.wait("tensor", BK.wait_for([2 * b, 2 * b + 1])
                           + [(xsem, qk_tgt)])
                    for half in range(2):
                        P.op("tensor", lambda t, qi=qi, kq=kq, b=b,
                             half=half, qT=qT, kT=kT: t.matmul(
                                 pall[:, 2 * b + half, :],
                                 qT[:, qi * 128:(qi + 1) * 128],
                                 kT[:, kq * 1024 + half * 512:
                                    kq * 1024 + (half + 1) * 512],
                                 start=True, stop=True,
                                 skip_group_check=True),
                             inc=[("c_t", 1)] if half == 1 else ())
                    tsc = P.val("c_t")
                    ew = [("c_t", max(tsc, slot_free.get(slot, 0)))]
                    P.op("scalar", lambda s, slot=slot, kq=kq, b=b:
                         s.activation(
                             out=Ebuf[:, slot, kq * 1024:(kq + 1) * 1024],
                             in_=pall[:, 2 * b:2 * b + 2, :], func=AF.Exp,
                             scale=QSCALE,
                             accum_out=lparts[:, slot * 4 + kq:
                                              slot * 4 + kq + 1]),
                         wait=ew + [("c_v", lp_free.get(slot, 0))],
                         chain=False)
                    BK.set([2 * b, 2 * b + 1], "c_s", P.val("c_s"))
                P.op("vector", lambda v, slot=slot: v.tensor_reduce(
                    out=lsum[:, slot:slot + 1],
                    in_=lparts[:, slot * 4:slot * 4 + 4],
                    axis=AX.X, op=ALU.add),
                    wait=[("c_s", P.val("c_s"))], chain=False)
                lp_free[slot] = P.val("c_v")
                P.op("vector", lambda v, slot=slot: v.reciprocal(
                    out=lr[:, slot:slot + 1], in_=lsum[:, slot:slot + 1]))
                P.op("vector", lambda v, slot=slot: v.tensor_scalar(
                    out=rbc[:, slot, :], in0=ones_bf[:],
                    scalar1=lr[:, slot:slot + 1],
                    scalar2=None, op0=ALU.mult),
                    wait=[("c_t", rbc_free.get(slot, 0))])
                emit_colsum_piece(2)
            return slots

        for item in items:
            slots = emit_scores(item)
            assert state["pending"] is None
            state["pending"] = (item, slots, 0, P.val("c_v"))
        while state["pending"] is not None:
            emit_colsum_piece(2)
        P.flush_block()


def _phase3_outproj(nc, P, pall, wo_d, val_d, attn_d, x_d, r1_d, h2_d, eps_t):
    with nc.sbuf_tensor("wo_sb", [128, NCH, D], BF16) as wo_sb, \
         nc.sbuf_tensor("vslab", [128, 2, NCH, 512], BF16) as vslab, \
         nc.sbuf_tensor("attn_sb", [128, 1, NCH, 512], BF16) as attn_sb, \
         nc.sbuf_tensor("attnT", [128, 2, 4, D], BF16) as attnT, \
         nc.sbuf_tensor("p3xt2", [128, 2, D], F32) as xt2, \
         nc.sbuf_tensor("p3y1sb", [128, D], F32) as y1sb, \
         nc.sbuf_tensor("p3r1sb", [128, 2, D], F32) as r1sb, \
         nc.sbuf_tensor("r1ld", [128, D], F32) as r1ld, \
         nc.sbuf_tensor("p3h2sb", [128, 2, D], BF16) as h2sb, \
         nc.sbuf_tensor("p3bn2", [128, 1, 4, 6], F32) as bn2, \
         nc.sbuf_tensor("p3mvb1", [128, 2, 4, 2], F32) as mvb1, \
         nc.sbuf_tensor("p3mvb2", [128, 2, 4, 2], F32) as mvb2, \
         nc.sbuf_tensor("p3sqb1", [128, 4], F32) as sqb1, \
         nc.sbuf_tensor("p3rstd1", [128, 4], F32) as rstd1, \
         nc.sbuf_tensor("p3nmb1", [128, 4], F32) as nmb1, \
         nc.sbuf_tensor("p3sqb2", [128, 4], F32) as sqb2, \
         nc.sbuf_tensor("p3rstd2", [128, 4], F32) as rstd2, \
         nc.sbuf_tensor("p3nmb2", [128, 4], F32) as nmb2:

        for vf in range(NCH):
            P.dma("gpsimd", lambda g, vf=vf: g.dma_start(
                out=wo_sb[:, vf, :], in_=wo_d[vf * 128:(vf + 1) * 128, :]),
                "d_wa", batch=(vf > 0))
        wo_tgt = P.val("d_wa")

        BK = Banks()
        vload = {}
        attn_store = {}
        stats1 = {}
        stats2 = {}
        mid_v = {}
        xt2_free = {0: 0, 1: 0}
        r1st_free = {0: 0, 1: 0}
        r1ld_free = {0: 0, 1: 0}
        h2st_free = {0: 0, 1: 0}
        r1st_done = {}

        def fold_front(tc):
            st = tc % 2
            psem = "d_sa" if tc % 2 == 0 else "d_sb"
            for j in range(4):
                tt = tc * 4 + j
                w = ([(psem, attn_store[tc]), ("c_v", mid_v.get(tc - 2, 0))]
                     if j == 0 else [])
                P.dma("scalar", lambda s, tt=tt, st=st, j=j: s.dma_start(
                    out=attnT[:, st, j, :],
                    in_=attn_d[:, tt * 128:(tt + 1) * 128], transpose=True),
                    "d_tr", wait=w, batch=(j > 0))
            tr = P.val("d_tr")
            P.wait("vector", [("d_tr", tr)])
            for j in range(4):
                for q in range(4):
                    P.op("vector", lambda v, st=st, j=j, q=q: v.bn_stats(
                        out=bn2[:, 0, q, :],
                        in_=attnT[:, st, j, q * 512:(q + 1) * 512]),
                        chain=(q == 0))
                P.op("vector", lambda v, st=st, j=j: v.bn_aggr(
                    out=mvb1[:, st, j, :], in_=bn2[:, 0, :, :]))
            stats1[tc] = P.val("c_v")

        def fold_mid(tc):
            st = tc % 2
            P.op("scalar", lambda s, st=st: s.activation(
                out=sqb1[:], in_=mvb1[:, st, :, 1], func=AF.Sqrt,
                bias=eps_t[:], scale=1.0), wait=[("c_v", stats1[tc])])
            sq = P.val("c_s")
            P.op("vector", lambda v: v.reciprocal(out=rstd1[:], in_=sqb1[:]),
                 wait=[("c_s", sq)])
            P.op("vector", lambda v, st=st: v.scalar_tensor_tensor(
                out=nmb1[:], in0=mvb1[:, st, :, 0], scalar=-1.0, in1=rstd1[:],
                op0=ALU.mult, op1=ALU.mult))
            for j in range(4):
                tt = tc * 4 + j
                jp = j % 2
                P.dma("gpsimd", lambda g, tt=tt, jp=jp: g.dma_start(
                    out=xt2[:, jp, :], in_=x_d[tt * 128:(tt + 1) * 128, :]),
                    "d_xa", wait=[("c_v", xt2_free[jp])])
                x_t = P.val("d_xa")
                P.op("vector", lambda v, jp=jp, j=j: v.tensor_scalar(
                    out=y1sb[:], in0=xt2[:, jp, :], scalar1=nmb1[:, j:j + 1],
                    scalar2=None, op0=ALU.add), wait=[("d_xa", x_t)])
                xt2_free[jp] = P.val("c_v")
                P.op("vector", lambda v, st=st, j=j, jp=jp:
                     v.scalar_tensor_tensor(
                         out=r1sb[:, jp, :], in0=attnT[:, st, j, :],
                         scalar=rstd1[:, j:j + 1], in1=y1sb[:],
                         op0=ALU.mult, op1=ALU.add),
                     wait=[("d_wv", r1st_free[jp])])
                P.dma("sync", lambda sy, tt=tt, jp=jp: sy.dma_start(
                    out=r1_d[tt * 128:(tt + 1) * 128, :], in_=r1sb[:, jp, :]),
                    "d_wv", wait=[("c_v", P.val("c_v"))])
                r1st_free[jp] = P.val("d_wv")
                for q in range(4):
                    P.op("vector", lambda v, jp=jp, q=q: v.bn_stats(
                        out=bn2[:, 0, q, :],
                        in_=r1sb[:, jp, q * 512:(q + 1) * 512]),
                        chain=(q == 0))
                P.op("vector", lambda v, st=st, j=j: v.bn_aggr(
                    out=mvb2[:, st, j, :], in_=bn2[:, 0, :, :]))
            mid_v[tc] = P.val("c_v")
            stats2[tc] = P.val("c_v")
            r1st_done[tc] = P.val("d_wv")

        def fold_tail(tc):
            st = tc % 2
            P.op("scalar", lambda s, st=st: s.activation(
                out=sqb2[:], in_=mvb2[:, st, :, 1], func=AF.Sqrt,
                bias=eps_t[:], scale=1.0), wait=[("c_v", stats2[tc])])
            sq = P.val("c_s")
            P.op("vector", lambda v: v.reciprocal(out=rstd2[:], in_=sqb2[:]),
                 wait=[("c_s", sq)])
            P.op("vector", lambda v, st=st: v.scalar_tensor_tensor(
                out=nmb2[:], in0=mvb2[:, st, :, 0], scalar=-1.0, in1=rstd2[:],
                op0=ALU.mult, op1=ALU.mult))
            for j in range(4):
                tt = tc * 4 + j
                jp = j % 2
                w = [("c_v", r1ld_free[0])]
                if j == 0:
                    w.append(("d_wv", r1st_done[tc]))
                P.dma("gpsimd", lambda g, tt=tt: g.dma_start(
                    out=r1ld[:], in_=r1_d[tt * 128:(tt + 1) * 128, :]),
                    "d_xb", wait=w)
                rt = P.val("d_xb")
                P.op("vector", lambda v, jp=jp, j=j: v.tensor_scalar(
                    out=h2sb[:, jp, :], in0=r1ld[:],
                    scalar1=rstd2[:, j:j + 1], scalar2=nmb2[:, j:j + 1],
                    op0=ALU.mult, op1=ALU.add),
                    wait=[("d_xb", rt), ("d_sg", h2st_free[jp])])
                r1ld_free[0] = P.val("c_v")
                P.dma("sync", lambda sy, tt=tt, jp=jp: sy.dma_start(
                    out=h2_d[tt * 128:(tt + 1) * 128, :], in_=h2sb[:, jp, :]),
                    "d_sg", wait=[("c_v", P.val("c_v"))])
                h2st_free[jp] = P.val("d_sg")

        def load_vslab(tc):
            P.dma("sync", lambda sy, tc=tc: sy.dma_start(
                out=vslab[:, tc % 2, 0:8, :],
                in_=val_d[0:1024, tc * 512:(tc + 1) * 512]
                .rearrange("(v p) t -> p v t", p=128)),
                "d_sv", wait=[("c_t", vload.get(tc - 2, 0))], batch=(tc > 0))
            P.dma("gpsimd", lambda sc, tc=tc: sc.dma_start(
                out=vslab[:, tc % 2, 8:16, :],
                in_=val_d[1024:2048, tc * 512:(tc + 1) * 512]
                .rearrange("(v p) t -> p v t", p=128)),
                "d_sv", batch=True, wait=[("c_t", vload.get(tc - 2, 0))])
            return P.val("d_sv")

        vs_tgt = {0: load_vslab(0)}
        for tc in range(NTC):
            vl_tgt = vs_tgt[tc]
            ssem2 = "d_sa" if tc % 2 == 0 else "d_sb"
            for fo in range(NCH):
                bank = fo % 4
                tw = [("d_sv", vl_tgt)] + BK.wait_for([bank])
                if tc == 0 and fo == 0:
                    tw.append(("d_wa", wo_tgt))
                P.wait("tensor", tw)
                for vf in range(NCH):
                    P.op("tensor", lambda t, vf=vf, fo=fo, tc=tc, bank=bank:
                         t.matmul(
                             pall[:, bank, :],
                             wo_sb[:, vf, fo * 128:(fo + 1) * 128],
                             vslab[:, tc % 2, vf, :],
                             start=(vf == 0), stop=(vf == NCH - 1),
                             skip_group_check=True),
                         inc=[("c_t", 1)] if vf == NCH - 1 else ())
                tpo = P.val("c_t")
                if fo == NCH - 1:
                    vload[tc] = tpo
                P.op("scalar", lambda s, fo=fo, bank=bank: s.activation(
                    out=attn_sb[:, 0, fo, :], in_=pall[:, bank, :],
                    func=AF.Copy),
                    wait=[("c_t", tpo),
                          (("d_sb" if tc % 2 == 0 else "d_sa"),
                           attn_store.get(tc - 1, 0))],
                    chain=False)
                BK.set([bank], "c_s", P.val("c_s"))
            attn_evac = P.val("c_s")
            P.dma("sync", lambda sy, tc=tc: sy.dma_start(
                out=attn_d[0:1024, tc * 512:(tc + 1) * 512]
                .rearrange("(f p) t -> p f t", p=128),
                in_=attn_sb[:, 0, 0:8, :]),
                ssem2, wait=[("c_s", attn_evac)])
            P.dma("gpsimd", lambda sc, tc=tc: sc.dma_start(
                out=attn_d[1024:2048, tc * 512:(tc + 1) * 512]
                .rearrange("(f p) t -> p f t", p=128),
                in_=attn_sb[:, 0, 8:16, :]),
                ssem2, batch=True, wait=[("c_s", attn_evac)])
            attn_store[tc] = P.val(ssem2)
            if tc + 1 < NTC:
                vs_tgt[tc + 1] = load_vslab(tc + 1)
            if tc >= 1:
                fold_mid(tc - 1)
            if tc >= 2:
                fold_tail(tc - 2)
            fold_front(tc)
        fold_mid(NTC - 1)
        fold_tail(NTC - 2)
        fold_tail(NTC - 1)
        P.flush_block()


def _phase4_ffn(nc, P, pall, sgt, sut, sdt, wg_d, wu_d, wd_d, h2_d, down_d,
                r1_d, out_d, eps_t):
    with nc.sbuf_tensor("h2T", [128, 2, NCH, 512], BF16) as h2T, \
         nc.sbuf_tensor("wg_sb", [128, 3, NCH, 256], BF16) as wg_sb, \
         nc.sbuf_tensor("wu_sb", [128, 3, NCH, 256], BF16) as wu_sb, \
         nc.sbuf_tensor("wd_sb", [128, 2, 2, D], BF16) as wd_sb, \
         nc.sbuf_tensor("fuse", [128, NI, 512], BF16) as fuse, \
         nc.sbuf_tensor("sgl", [128, 2, 512], BF16) as sgl, \
         nc.sbuf_tensor("down_sb", [128, 2, 2, 512], BF16) as down_sb, \
         nc.sbuf_tensor("downT", [128, 4, D], BF16) as downT, \
         nc.sbuf_tensor("r1t", [128, D], F32) as r1t, \
         nc.sbuf_tensor("y2sb", [128, D], F32) as y2sb, \
         nc.sbuf_tensor("outsb", [128, D], F32) as outsb, \
         nc.sbuf_tensor("bn3", [128, 1, 4, 6], F32) as bn3, \
         nc.sbuf_tensor("mvb3", [128, 4, 2], F32) as mvb3, \
         nc.sbuf_tensor("sqb3", [128, 4], F32) as sqb3, \
         nc.sbuf_tensor("rstdb3", [128, 4], F32) as rstdb3, \
         nc.sbuf_tensor("nmbb3", [128, 4], F32) as nmbb3:

        NHG = 32
        BK = Banks()
        gu_done = {}
        wgu_free = {0: 0, 1: 0, 2: 0}
        wd_free = {0: 0, 1: 0}
        sgl_free = {0: 0, 1: 0}
        fuse_done = {}
        down_read = {}
        down_store_tc = {}
        down_store2 = {}

        preloaded = {}
        gctr = {"n": 0}
        epi = {"apply_v": 0, "stats_v": 0, "st": 0, "tr": 0}

        def load_wgu(hg):
            gb = gctr["n"] % 3
            gctr["n"] += 1
            P.dma("gpsimd", lambda gg, hg=hg, gb=gb: gg.dma_start(
                out=wg_sb[:, gb, :, :], in_=wg_d[:, hg, :, :]),
                "d_wa", wait=[("c_t", wgu_free[gb])])
            P.dma("gpsimd", lambda gg, hg=hg, gb=gb: gg.dma_start(
                out=wu_sb[:, gb, :, :], in_=wu_d[:, hg, :, :]),
                "d_wa", batch=True)
            return "d_wa", P.val("d_wa"), gb

        def emit_epilogue_front(tc):
            for j in range(4):
                tt = tc * 4 + j
                w = ([("d_sg", down_store_tc[tc]),
                      ("c_v", epi["apply_v"])] if j == 0 else [])
                P.dma("scalar", lambda s, tt=tt, j=j: s.dma_start(
                    out=downT[:, j, :],
                    in_=down_d[:, tt * 128:(tt + 1) * 128],
                    transpose=True), "d_xa", wait=w, batch=(j > 0))
            epi["tr"] = P.val("d_xa")
            P.wait("vector", [("d_xa", epi["tr"])])
            for j in range(4):
                for q in range(4):
                    P.op("vector", lambda v, j=j, q=q: v.bn_stats(
                        out=bn3[:, 0, q, :],
                        in_=downT[:, j, q * 512:(q + 1) * 512]),
                        chain=(j == 0 and q == 0))
                P.op("vector", lambda v, j=j: v.bn_aggr(
                    out=mvb3[:, j, :], in_=bn3[:, 0, :, :]))
            epi["stats_v"] = P.val("c_v")

        def emit_epilogue_back(tc):
            P.op("scalar", lambda s: s.activation(
                out=sqb3[:], in_=mvb3[:, :, 1], func=AF.Sqrt, bias=eps_t[:],
                scale=1.0), wait=[("c_v", epi["stats_v"])])
            sq_s = P.val("c_s")
            P.op("vector", lambda v: v.reciprocal(out=rstdb3[:], in_=sqb3[:]),
                 wait=[("c_s", sq_s)])
            P.op("vector", lambda v: v.scalar_tensor_tensor(
                out=nmbb3[:], in0=mvb3[:, :, 0], scalar=-1.0, in1=rstdb3[:],
                op0=ALU.mult, op1=ALU.mult))
            for j in range(4):
                tt = tc * 4 + j
                P.dma("sync", lambda sy, tt=tt: sy.dma_start(
                    out=r1t[:], in_=r1_d[tt * 128:(tt + 1) * 128, :]),
                    "d_xb", wait=[("c_v", epi["apply_v"])])
                r1_tgt = P.val("d_xb")
                P.op("vector", lambda v, j=j: v.tensor_scalar(
                    out=y2sb[:], in0=r1t[:], scalar1=nmbb3[:, j:j + 1],
                    scalar2=None, op0=ALU.add),
                    wait=[("d_xb", r1_tgt), ("d_sa", epi["st"])])
                P.op("vector", lambda v, j=j: v.scalar_tensor_tensor(
                    out=outsb[:], in0=downT[:, j, :],
                    scalar=rstdb3[:, j:j + 1], in1=y2sb[:],
                    op0=ALU.mult, op1=ALU.add))
                epi["apply_v"] = P.val("c_v")
                P.dma("sync", lambda sy, tt=tt: sy.dma_start(
                    out=out_d[tt * 128:(tt + 1) * 128, :], in_=outsb[:]),
                    "d_sa", wait=[("c_v", epi["apply_v"])])
                epi["st"] = P.val("d_sa")

        def emit_h2T(tc):
            tb = tc % 2
            for c in range(NCH):
                tr_w = ([("d_tr", P.val("d_tr")),
                         ("c_t", gu_done.get(tc - 2, 0))] if c == 0 else [])
                P.dma("sync", lambda s, tc=tc, tb=tb, c=c: s.dma_start(
                    out=h2T[:, tb, c, :],
                    in_=h2_d[tc * 512:(tc + 1) * 512, c * 128:(c + 1) * 128],
                    transpose=True),
                    "d_tr", wait=tr_w, batch=(c > 0))
            return P.val("d_tr")

        h2T_tgts = {}
        for tc in range(NTC):
            tb = tc % 2
            if tc not in h2T_tgts:
                h2T_tgts[tc] = emit_h2T(tc)
            h2T_tgt = h2T_tgts[tc]
            if tc >= 1:
                emit_epilogue_front(tc - 1)

            for hg in range(NHG):
                for la in (hg, hg + 1, hg + 2):
                    if la < NHG and (tc, la) not in preloaded:
                        preloaded[(tc, la)] = load_wgu(la)
                gsem, wg_tgt, gb = preloaded.pop((tc, hg))
                for i2 in range(2):
                    i = hg * 2 + i2
                    pg = (0, 1, 4)[i % 3]
                    pu = (2, 3, 5)[i % 3]
                    P.wait("tensor", [(gsem, wg_tgt), ("d_tr", h2T_tgt)]
                           + BK.wait_for([pg, pu]))
                    for c in range(NCH):
                        P.op("tensor", lambda t, c=c, i2=i2, gb=gb, tb=tb,
                             pg=pg: t.matmul(
                                 pall[:, pg, :],
                                 wg_sb[:, gb, c, i2 * 128:(i2 + 1) * 128],
                                 h2T[:, tb, c, :],
                                 start=(c == 0), stop=(c == NCH - 1),
                                 skip_group_check=True),
                             inc=[("c_t", 1)] if c == NCH - 1 else ())
                    tg = P.val("c_t")
                    for c in range(NCH):
                        P.op("tensor", lambda t, c=c, i2=i2, gb=gb, tb=tb,
                             pu=pu: t.matmul(
                                 pall[:, pu, :],
                                 wu_sb[:, gb, c, i2 * 128:(i2 + 1) * 128],
                                 h2T[:, tb, c, :],
                                 start=(c == 0), stop=(c == NCH - 1),
                                 skip_group_check=True),
                             inc=[("c_t", 1)] if c == NCH - 1 else ())
                    tu = P.val("c_t")
                    P.op("scalar", lambda s, i=i, pg=pg: s.activation(
                        out=sgl[:, i % 2, :], in_=pall[:, pg, :],
                        func=AF.Silu, scale=sgt[:, i:i + 1]),
                        wait=[("c_t", tg), ("c_v", sgl_free[i % 2])],
                        chain=False)
                    BK.set([pg], "c_s", P.val("c_s"))
                    P.op("vector", lambda v, i=i, pu=pu: v.scalar_tensor_tensor(
                        out=fuse[:, i, :], in0=pall[:, pu, :],
                        scalar=sut[:, i:i + 1], in1=sgl[:, i % 2, :],
                        op0=ALU.mult, op1=ALU.mult),
                        wait=[("c_t", tu), ("c_s", P.val("c_s")),
                              ("c_t", down_read.get(tc - 1, 0))],
                        chain=False)
                    BK.set([pu], "c_v", P.val("c_v"))
                    sgl_free[i % 2] = P.val("c_v")
                wgu_free[gb] = P.val("c_t")
            gu_done[tc] = P.val("c_t")
            fuse_done[tc] = P.val("c_v")
            if tc >= 1:
                emit_epilogue_back(tc - 1)
            if tc + 1 < NTC:
                for hg in range(2):
                    preloaded[(tc + 1, hg)] = load_wgu(hg)

            for p in range(2):
                if p == 1 and tc + 1 < NTC:
                    h2T_tgts[tc + 1] = emit_h2T(tc + 1)
                for s2 in range(32):
                    wdb = s2 % 2
                    dsem = "d_wb" if wdb == 0 else "d_wv"
                    P.dma("gpsimd", lambda gg, s2=s2, wdb=wdb: gg.dma_start(
                        out=wd_sb[:, wdb, :, :], in_=wd_d[:, s2, :, :]),
                        dsem, wait=[("c_t", wd_free[wdb])])
                    wd_tgt = P.val(dsem)
                    tw = [(dsem, wd_tgt), ("c_v", fuse_done[tc])]
                    if s2 == 0:
                        tw += BK.wait_for(list(range(8)))
                        tw.append(("c_s", P.val("c_s")))
                    P.wait("tensor", tw)
                    for q in range(2):
                        i = s2 * 2 + q
                        for f8 in range(8):
                            fo = p * 8 + f8
                            P.op("tensor", lambda t, q=q, f8=f8, fo=fo, i=i,
                                 wdb=wdb: t.matmul(
                                     pall[:, f8, :],
                                     wd_sb[:, wdb, q, fo * 128:(fo + 1) * 128],
                                     fuse[:, i, :],
                                     start=(i == 0), stop=(i == NI - 1),
                                     skip_group_check=True),
                                 inc=[("c_t", 1)]
                                 if (q == 1 and f8 == 7) else ())
                    wd_free[wdb] = P.val("c_t")
                tdn = P.val("c_t")
                for grp in range(4):
                    hb = grp % 2
                    for f2 in range(2):
                        f8 = grp * 2 + f2
                        fo = p * 8 + f8
                        P.op("scalar", lambda s, f2=f2, f8=f8, fo=fo,
                             hb=hb: s.activation(
                                 out=down_sb[:, hb, f2, :],
                                 in_=pall[:, f8, :],
                                 func=AF.Copy, scale=sdt[:, fo:fo + 1]),
                             wait=[("c_t", tdn),
                                   ("d_sg", down_store2.get(hb, 0))],
                             chain=False)
                    BK.set([grp * 2, grp * 2 + 1], "c_s", P.val("c_s"))
                    P.dma("sync", lambda sy, tc=tc, p=p, grp=grp, hb=hb:
                          sy.dma_start(
                              out=down_d[(p * 8 + grp * 2) * 128:
                                         (p * 8 + grp * 2 + 2) * 128,
                                         tc * 512:(tc + 1) * 512]
                              .rearrange("(f q) t -> q f t", q=128),
                              in_=down_sb[:, hb, :, :]),
                          "d_sg", wait=[("c_s", P.val("c_s"))])
                    down_store2[hb] = P.val("d_sg")
            down_store_tc[tc] = P.val("d_sg")
            down_read[tc] = P.val("c_t")
        emit_epilogue_front(NTC - 1)
        emit_epilogue_back(NTC - 1)
        P.flush_block()


def kernel(L0_attn_x, L0_attn_w_qkv, L0_attn_w_o, L0_ffn_w_gate, L0_ffn_w_up,
           L0_ffn_w_down, L0_ffn_s_gate, L0_ffn_s_up, L0_ffn_s_down):
    nc = build()
    x = np.ascontiguousarray(np.asarray(L0_attn_x, dtype=np.float32))
    shared = {
        "wqkv": np.ascontiguousarray(np.asarray(L0_attn_w_qkv, dtype=np.float32)),
        "wo": np.ascontiguousarray(np.asarray(L0_attn_w_o, dtype=np.float32)),
        "wg": np.ascontiguousarray(np.asarray(L0_ffn_w_gate, dtype=np.int8)
                                   .reshape(16, 128, 32, 256)
                                   .transpose(1, 2, 0, 3)),
        "wu": np.ascontiguousarray(np.asarray(L0_ffn_w_up, dtype=np.int8)
                                   .reshape(16, 128, 32, 256)
                                   .transpose(1, 2, 0, 3)),
        "wd": np.ascontiguousarray(np.asarray(L0_ffn_w_down, dtype=np.int8)
                                   .reshape(32, 2, 128, 2048)
                                   .transpose(2, 0, 1, 3)),
        "sg": np.ascontiguousarray(np.asarray(L0_ffn_s_gate, dtype=np.float32).reshape(NI, 128).T),
        "su": np.ascontiguousarray(np.asarray(L0_ffn_s_up, dtype=np.float32).reshape(NI, 128).T),
        "sd": np.ascontiguousarray(np.asarray(L0_ffn_s_down, dtype=np.float32).reshape(NCH, 128).T),
    }
    in_maps = [dict(shared, x=np.ascontiguousarray(x[b])) for b in range(8)]
    res = run_bass_kernel_spmd(nc, in_maps, core_ids=list(range(8)))
    return np.stack([res.results[b]["out"] for b in range(8)], axis=0)


# version 25
# speedup vs baseline: 1.3112x; 1.0645x over previous
import numpy as np
from contextlib import ExitStack

import concourse.bass as bass
import concourse.mybir as mybir
from concourse.bass_utils import run_bass_kernel_spmd

F32 = mybir.dt.float32
BF16 = mybir.dt.bfloat16
I8 = mybir.dt.int8
AF = mybir.ActivationFunctionType
ALU = mybir.AluOpType
AX = mybir.AxisListType

S_TOK = 4096
D = 2048
H = 16
INTER = 8192
NT = S_TOK // 128
NCH = D // 128
NI = INTER // 128
NTC = 8
EPS = 1e-5
QSCALE = float(1.0 / np.sqrt(128.0))
GRP = 4

ENGS = ("tensor", "scalar", "vector", "sync", "gpsimd")
SEM_NAMES = ("d_xa", "d_xb", "d_sa", "d_sb", "d_sv", "d_sg", "d_wa", "d_wb",
             "d_wv", "d_tr", "c_t", "c_s", "c_v", "c_g")


class Prog:
    def __init__(self, nc):
        self.nc = nc
        self.ops = {e: [] for e in ENGS}
        self.cnt = {s: 0 for s in SEM_NAMES}
        self.stack = ExitStack()
        self.S = {s: self.stack.enter_context(nc.semaphore(s)) for s in SEM_NAMES}

    def _emit(self, eng, fn, waits, incs):
        S = self.S
        waits = tuple((s, t) for s, t in waits if t > 0)
        incs = tuple(incs)

        def run(e):
            for s, t in waits:
                e.wait_ge(S[s], t)
            ins = fn(e)
            for s, n in incs:
                ins.then_inc(S[s], n)
            return ins

        self.ops[eng].append(run)

    def op(self, eng, fn, wait=(), inc=(), chain=True):
        waits = list(wait)
        incs = list(inc)
        if eng == "scalar":
            assert not incs
            if chain:
                waits.append(("c_s", self.cnt["c_s"]))
            self.cnt["c_s"] += 1
            incs.append(("c_s", 1))
        elif eng == "vector":
            assert not incs
            if chain:
                waits.append(("c_v", self.cnt["c_v"]))
            self.cnt["c_v"] += 1
            incs.append(("c_v", 1))
        else:
            for s, n in incs:
                self.cnt[s] += n
        self._emit(eng, fn, waits, incs)

    def dma(self, eng, fn, sem, wait=(), batch=False):
        waits = list(wait)
        if not batch:
            waits.append((sem, self.cnt[sem]))
        self.cnt[sem] += 16
        self._emit(eng, fn, waits, [(sem, 16)])

    def wait(self, eng, wait):
        S = self.S
        waits = tuple((s, t) for s, t in wait if t > 0)
        if not waits:
            return

        def run(e):
            for s, t in waits:
                e.wait_ge(S[s], t)

        self.ops[eng].append(run)

    def val(self, s):
        return self.cnt[s]

    def phase_barrier(self):
        w = [(s, self.cnt[s]) for s in SEM_NAMES]
        for e in ENGS:
            self.wait(e, w)

    def flush_block(self):
        ops = self.ops
        with self.nc.Block() as b:
            @b.tensor
            def _(t):
                for f in ops["tensor"]:
                    f(t)

            @b.scalar
            def _(s):
                for f in ops["scalar"]:
                    f(s)

            @b.vector
            def _(v):
                for f in ops["vector"]:
                    f(v)

            @b.sync
            def _(sy):
                for f in ops["sync"]:
                    f(sy)

            @b.gpsimd
            def _(g):
                for f in ops["gpsimd"]:
                    f(g)
        self.ops = {e: [] for e in ENGS}



def _batched_ln(P, nb, tiles_ap, bnbuf, mvb, sqb, rstdb, nmbb, eps_t):
    for j in range(nb):
        for q in range(4):
            P.op("vector", lambda v, j=j, q=q, t=tiles_ap: v.bn_stats(
                out=bnbuf[:, j, q, :], in_=t(j)[:, q * 512:(q + 1) * 512]),
                chain=(j == 0 and q == 0))
    for j in range(nb):
        P.op("vector", lambda v, j=j: v.bn_aggr(
            out=mvb[:, j, :], in_=bnbuf[:, j, :, :]), chain=(j == 0))
    P.op("scalar", lambda s: s.activation(
        out=sqb[:], in_=mvb[:, :, 1], func=AF.Sqrt, bias=eps_t[:], scale=1.0),
        wait=[("c_v", P.val("c_v"))])
    P.op("vector", lambda v: v.reciprocal(out=rstdb[:], in_=sqb[:]),
         wait=[("c_s", P.val("c_s"))])
    P.op("vector", lambda v: v.scalar_tensor_tensor(
        out=nmbb[:], in0=mvb[:, :, 0], scalar=-1.0, in1=rstdb[:],
        op0=ALU.mult, op1=ALU.mult))


def _ln_stats(P, tile_ap, bnbuf, mv):
    for j in range(4):
        P.op("vector", lambda v, t=tile_ap, b=bnbuf, j=j: v.bn_stats(
            out=b[:, j, :], in_=t[:, j * 512:(j + 1) * 512]))
    P.op("vector", lambda v, b=bnbuf, m=mv: v.bn_aggr(out=m[:], in_=b[:]))


def _ln_scalebias(P, mv, sq, rstd, nmb, eps_t):
    P.op("scalar", lambda s, m=mv, q=sq, e=eps_t: s.activation(
        out=q[:], in_=m[:, 1:2], func=AF.Sqrt, bias=e[:], scale=1.0),
        wait=[("c_v", P.val("c_v"))])
    P.op("vector", lambda v, q=sq, r=rstd: v.reciprocal(out=r[:], in_=q[:]),
         wait=[("c_s", P.val("c_s"))])
    P.op("vector", lambda v, m=mv, r=rstd, o=nmb: v.tensor_scalar(
        out=o[:], in0=m[:, 0:1], scalar1=r[:], scalar2=-1.0,
        op0=ALU.mult, op1=ALU.mult))


class Banks:

    def __init__(self):
        self.free = {b: ("c_s", 0) for b in range(8)}

    def wait_for(self, banks):
        return [self.free[b] for b in banks]

    def set(self, banks, sem, tgt):
        for b in banks:
            self.free[b] = (sem, tgt)


def build(debug=0, max_phase=5):
    nc = bass.Bass("TRN2", target_bir_lowering=False)

    x_d = nc.dram_tensor("x", [S_TOK, D], F32, kind="ExternalInput")
    wqkv_d = nc.dram_tensor("wqkv", [D, 3 * D], F32, kind="ExternalInput")
    wo_d = nc.dram_tensor("wo", [D, D], F32, kind="ExternalInput")
    wg_d = nc.dram_tensor("wg", [128, 32, NCH, 256], I8, kind="ExternalInput")
    wu_d = nc.dram_tensor("wu", [128, 32, NCH, 256], I8, kind="ExternalInput")
    wd_d = nc.dram_tensor("wd", [128, 32, 2, D], I8, kind="ExternalInput")
    sg_d = nc.dram_tensor("sg", [128, NI], F32, kind="ExternalInput")
    su_d = nc.dram_tensor("su", [128, NI], F32, kind="ExternalInput")
    sd_d = nc.dram_tensor("sd", [128, NCH], F32, kind="ExternalInput")
    out_d = nc.dram_tensor("out", [S_TOK, D], F32, kind="ExternalOutput")

    sk = "ExternalOutput" if debug else "Internal"
    h1_d = nc.dram_tensor("h1d", [S_TOK, D], BF16, kind=sk)
    val_d = nc.dram_tensor("vald", [D, S_TOK], BF16, kind=sk)
    attn_d = nc.dram_tensor("attnd", [D, S_TOK], BF16, kind=sk)
    r1_d = nc.dram_tensor("r1d", [S_TOK, D], F32, kind=sk)
    h2_d = nc.dram_tensor("h2d", [S_TOK, D], BF16, kind=sk)
    down_d = nc.dram_tensor("downd", [D, S_TOK], BF16, kind=sk)
    qkv_d = nc.dram_tensor("qkvd", [3 * D, S_TOK], BF16, kind=sk)

    es = ExitStack()
    P = Prog(nc)
    es.enter_context(P.stack)

    sgt = es.enter_context(nc.sbuf_tensor("sgt", [128, NI], F32))
    sut = es.enter_context(nc.sbuf_tensor("sut", [128, NI], F32))
    sdt = es.enter_context(nc.sbuf_tensor("sdt", [128, NCH], F32))
    ones_bf = es.enter_context(nc.sbuf_tensor("ones_bf", [128, 128], BF16))
    eps_t = es.enter_context(nc.sbuf_tensor("eps_t", [128, 1], F32))
    pall = es.enter_context(nc.psum_tensor("pall", [128, 8, 512], F32))

    P.dma("gpsimd", lambda g: g.dma_start(
        out=sgt[:], in_=sg_d[:]), "d_wa")
    P.dma("gpsimd", lambda g: g.dma_start(
        out=sut[:], in_=su_d[:]), "d_wa")
    P.dma("gpsimd", lambda g: g.dma_start(
        out=sdt[:], in_=sd_d[:]), "d_wa")
    P.op("gpsimd", lambda g: g.memset(ones_bf[:], 1.0), inc=[("c_g", 1)])
    P.op("gpsimd", lambda g: g.memset(eps_t[:], EPS), inc=[("c_g", 1)])
    for _e in ("scalar", "vector", "tensor"):
        P.wait(_e, [("c_g", 2)])

    h1T_scope = ExitStack()
    h1T = h1T_scope.enter_context(nc.sbuf_tensor("h1T", [128, NCH, S_TOK], BF16))

    NB1 = 4
    with nc.sbuf_tensor("xt", [128, NB1, D], F32) as xt, \
         nc.sbuf_tensor("h1sb", [128, NB1, D], BF16) as h1sb, \
         nc.sbuf_tensor("bnbuf", [128, NB1, 4, 6], F32) as bnbuf, \
         nc.sbuf_tensor("mv1", [128, NB1, 2], F32) as mv1, \
         nc.sbuf_tensor("sq1", [128, NB1], F32) as sq1, \
         nc.sbuf_tensor("rstd1", [128, NB1], F32) as rstd1, \
         nc.sbuf_tensor("nmb1", [128, NB1], F32) as nmb1:

        nbat = NT // NB1
        s_done = {}
        v_done = {}
        st_done = {}
        for bt in range(nbat):
            xsem = "d_xa" if bt % 2 == 0 else "d_xb"
            ssem = "d_sa" if bt % 2 == 0 else "d_sb"
            osem = "d_sb" if bt % 2 == 0 else "d_sa"
            for j in range(NB1):
                tt = bt * NB1 + j
                P.dma("sync", lambda sy, tt=tt, j=j: sy.dma_start(
                    out=xt[:, j, :], in_=x_d[tt * 128:(tt + 1) * 128, :]),
                    xsem, batch=(j > 0),
                    wait=[("c_s", s_done.get(bt - 1, 0))] if j == 0 else [])
            x_tgt = P.val(xsem)
            P.wait("vector", [(xsem, x_tgt), ("c_g", 2)])
            _batched_ln(P, NB1, lambda j: xt[:, j, :], bnbuf, mv1, sq1,
                        rstd1, nmb1, eps_t)
            ln_v = P.val("c_v")
            for j in range(NB1):
                w = [("c_v", ln_v)]
                if j == 0:
                    w.append((osem, st_done.get(bt - 1, 0)))
                P.op("scalar", lambda s, j=j: s.activation(
                    out=h1sb[:, j, :], in_=xt[:, j, :], func=AF.Identity,
                    scale=rstd1[:, j:j + 1], bias=nmb1[:, j:j + 1]),
                    wait=w, chain=(j == 0))
            s_done[bt] = P.val("c_s")
            for j in range(NB1):
                tt = bt * NB1 + j
                P.dma("sync", lambda sy, tt=tt, j=j: sy.dma_start(
                    out=h1_d[tt * 128:(tt + 1) * 128, :], in_=h1sb[:, j, :]),
                    ssem, batch=(j > 0),
                    wait=[("c_s", s_done[bt])] if j == 0 else [])
            st_done[bt] = P.val(ssem)
            v_done[bt] = P.val("c_v")

        st_w = [("d_sa", P.val("d_sa")), ("d_sb", P.val("d_sb"))]
        for c in range(NCH):
            P.dma("sync", lambda s, c=c: s.dma_start(
                out=h1T[:, c, :], in_=h1_d[:, c * 128:(c + 1) * 128],
                transpose=True),
                "d_tr", wait=st_w if c == 0 else [], batch=(c > 0))
        h1T_tgt = P.val("d_tr")
        P.flush_block()

    if max_phase >= 2:
        _phase2_qkv(nc, P, h1T, pall, wqkv_d, qkv_d, h1T_tgt)
        h1T_scope.close()
        P.phase_barrier()
        _phase2b_attn(nc, P, pall, ones_bf, qkv_d, val_d)
    else:
        h1T_scope.close()

    h2_done = None
    if max_phase >= 3:
        P.phase_barrier()
        h2_done = _phase3_outproj(nc, P, pall, wo_d, val_d, attn_d, x_d,
                                  r1_d, h2_d, eps_t)
    if max_phase >= 4:
        _phase4_ffn(nc, P, pall, sgt, sut, sdt, wg_d, wu_d, wd_d, h2_d, down_d,
                    r1_d, out_d, eps_t, h2_done)

    P.wait("sync", [("d_sa", P.val("d_sa")), ("d_sb", P.val("d_sb")),
                    ("d_sv", P.val("d_sv")), ("d_sg", P.val("d_sg"))])
    P.flush_block()
    es.close()
    return nc


def _phase2_qkv(nc, P, h1T, pall, wqkv_d, qkv_d, h1T_tgt):
    with nc.sbuf_tensor("wqkv_sb", [128, 2, NCH, 128], BF16) as wqkv_sb, \
         nc.sbuf_tensor("qkv_sb", [128, 2, S_TOK], BF16) as qkv_sb:
        BK = Banks()
        t_done = {}
        st_done = {0: 0, 1: 0}
        for f in range(48):
            fb = f % 2
            wsem = "d_wa" if fb == 0 else "d_wb"
            P.dma("gpsimd", lambda g, f=f, fb=fb: g.dma_start(
                out=wqkv_sb[:, fb, :, :],
                in_=wqkv_d[:, f * 128:(f + 1) * 128]
                .rearrange("(c p) x -> p c x", p=128)),
                wsem, wait=[("c_t", t_done.get(f - 2, 0))])
            w_tgt = P.val(wsem)
            ssem = "d_sa" if fb == 0 else "d_sb"
            for ts8 in range(8):
                bank = ts8 % 4
                tw = [(wsem, w_tgt)] + BK.wait_for([bank])
                if f == 0 and ts8 == 0:
                    tw.append(("d_tr", h1T_tgt))
                P.wait("tensor", tw)
                for c in range(NCH):
                    P.op("tensor", lambda t, fb=fb, c=c, ts8=ts8, bank=bank:
                         t.matmul(
                             pall[:, bank, :], wqkv_sb[:, fb, c, :],
                             h1T[:, c, ts8 * 512:(ts8 + 1) * 512],
                             start=(c == 0), stop=(c == NCH - 1),
                             skip_group_check=True),
                         inc=[("c_t", 1)] if c == NCH - 1 else ())
                tmm = P.val("c_t")
                P.op("vector", lambda v, fb=fb, ts8=ts8, bank=bank:
                     v.tensor_copy(
                         out=qkv_sb[:, fb, ts8 * 512:(ts8 + 1) * 512],
                         in_=pall[:, bank, :]),
                     wait=[("c_t", tmm), (ssem, st_done[fb])], chain=False)
                BK.set([bank], "c_v", P.val("c_v"))
            t_done[f] = P.val("c_t")
            P.dma("sync", lambda sy, f=f, fb=fb: sy.dma_start(
                out=qkv_d[f * 128:(f + 1) * 128, :], in_=qkv_sb[:, fb, :]),
                ssem, wait=[("c_v", P.val("c_v"))])
            st_done[fb] = P.val(ssem)
        P.flush_block()


def _phase2b_attn(nc, P, pall, ones_bf, qkv_d, val_d):
    NSLOT = 3 * GRP
    with nc.sbuf_tensor("qkT", [128, 2, 2, S_TOK], BF16) as qkT, \
         nc.sbuf_tensor("vTb", [128, 2, S_TOK], BF16) as vTb, \
         nc.sbuf_tensor("valsb", [128, 2, S_TOK], BF16) as valsb, \
         nc.sbuf_tensor("Ebuf", [128, NSLOT, S_TOK], BF16) as Ebuf, \
         nc.sbuf_tensor("cacc", [128, S_TOK], F32) as cacc, \
         nc.sbuf_tensor("rbc", [128, NSLOT, 128], BF16) as rbc, \
         nc.sbuf_tensor("lparts", [128, NSLOT * 4], F32) as lparts, \
         nc.sbuf_tensor("lsum", [128, NSLOT], F32) as lsum, \
         nc.sbuf_tensor("lr", [128, NSLOT], F32) as lr:

        groups = [list(range(g, g + GRP)) for g in range(0, NT, GRP)]
        NGRP = len(groups)
        items = [(h, gi, groups[gi]) for h in range(H) for gi in range(NGRP)]
        BK = Banks()
        val_store = {}
        scores_done = {}
        vmul_done = {}
        slot_free = {}
        rbc_free = {}
        lp_free = {}
        load_info = {}
        state = {"nq": 0, "pending": None}

        def emit_loads(h):
            hp = h % 2
            xsem = "d_xa" if hp == 0 else "d_xb"
            P.dma("sync", lambda sy, h=h, hp=hp: sy.dma_start(
                out=qkT[:, hp, 0, :], in_=qkv_d[h * 128:(h + 1) * 128, :]),
                xsem, wait=[("c_t", scores_done.get(h - 2, 0))])
            P.dma("sync", lambda sy, h=h, hp=hp: sy.dma_start(
                out=qkT[:, hp, 1, :],
                in_=qkv_d[(16 + h) * 128:(17 + h) * 128, :]),
                xsem, batch=True)
            qk_tgt = P.val(xsem)
            P.dma("sync", lambda sy, h=h, hp=hp: sy.dma_start(
                out=vTb[:, hp, :],
                in_=qkv_d[(32 + h) * 128:(33 + h) * 128, :]),
                xsem, wait=[("c_v", vmul_done.get(h - 2, 0))])
            load_info[h] = (xsem, qk_tgt, P.val(xsem))

        def emit_colsum_piece(nk8=2):
            if state["pending"] is None:
                return
            item, slots, k8_0, rbc_tgt = state["pending"]
            h, gi, qis = item
            first = (gi == 0)
            k8_end = min(k8_0 + nk8, 8)
            for k8 in range(k8_0, k8_end):
                pb = 6 + (k8 % 2)
                w = BK.wait_for([pb])
                if k8 == 0:
                    w = w + [("c_v", rbc_tgt)]
                P.wait("tensor", w)
                nsl = len(slots)
                for j, slot in enumerate(slots):
                    P.op("tensor", lambda t, slot=slot, k8=k8, pb=pb,
                         j=j, nsl=nsl: t.matmul(
                             pall[:, pb, :], rbc[:, slot, :],
                             Ebuf[:, slot, k8 * 512:(k8 + 1) * 512],
                             start=(j == 0), stop=(j == nsl - 1),
                             skip_group_check=True),
                         inc=[("c_t", 1)] if j == nsl - 1 else ())
                tcs = P.val("c_t")
                if first:
                    P.op("vector", lambda v, k8=k8, pb=pb: v.tensor_copy(
                        out=cacc[:, k8 * 512:(k8 + 1) * 512],
                        in_=pall[:, pb, :]),
                        wait=[("c_t", tcs)], chain=(k8 == 0))
                else:
                    P.op("vector", lambda v, k8=k8, pb=pb: v.tensor_tensor(
                        out=cacc[:, k8 * 512:(k8 + 1) * 512],
                        in0=pall[:, pb, :],
                        in1=cacc[:, k8 * 512:(k8 + 1) * 512], op=ALU.add),
                        wait=[("c_t", tcs)], chain=(k8 == 0))
                BK.set([pb], "c_v", P.val("c_v"))
            if k8_end == 8:
                cd = P.val("c_t")
                for slot in slots:
                    slot_free[slot] = cd
                    rbc_free[slot] = cd
                if gi == NGRP - 1:
                    hp = h % 2
                    xsem, _, v_tgt = load_info[h]
                    P.op("vector", lambda v, hp=hp: v.tensor_tensor(
                        out=valsb[:, hp, :], in0=vTb[:, hp, :], in1=cacc[:],
                        op=ALU.mult),
                        wait=[(xsem, v_tgt),
                              ("d_sv", val_store.get(h - 2, 0))])
                    vmul_done[h] = P.val("c_v")
                    P.dma("sync", lambda sy, h=h, hp=hp: sy.dma_start(
                        out=val_d[h * 128:(h + 1) * 128, :],
                        in_=valsb[:, hp, :]),
                        "d_sv", wait=[("c_v", vmul_done[h])])
                    val_store[h] = P.val("d_sv")
                    scores_done[h] = P.val("c_t")
                state["pending"] = None
            else:
                state["pending"] = (item, slots, k8_end, rbc_tgt)

        def emit_scores(item):
            h, gi, qis = item
            hp = h % 2
            if gi == 0:
                emit_loads(h)
            xsem, qk_tgt, _ = load_info[h]
            qT = qkT[:, hp, 0, :]
            kT = qkT[:, hp, 1, :]
            slots = [((h * NGRP + gi) % 3) * GRP + j for j in range(len(qis))]
            for slot, qi in zip(slots, qis):
                for kq in range(4):
                    b = state["nq"] % 3
                    state["nq"] += 1
                    P.wait("tensor", BK.wait_for([2 * b, 2 * b + 1])
                           + [(xsem, qk_tgt)])
                    for half in range(2):
                        P.op("tensor", lambda t, qi=qi, kq=kq, b=b,
                             half=half, qT=qT, kT=kT: t.matmul(
                                 pall[:, 2 * b + half, :],
                                 qT[:, qi * 128:(qi + 1) * 128],
                                 kT[:, kq * 1024 + half * 512:
                                    kq * 1024 + (half + 1) * 512],
                                 start=True, stop=True,
                                 skip_group_check=True),
                             inc=[("c_t", 1)] if half == 1 else ())
                    tsc = P.val("c_t")
                    ew = [("c_t", max(tsc, slot_free.get(slot, 0)))]
                    P.op("scalar", lambda s, slot=slot, kq=kq, b=b:
                         s.activation(
                             out=Ebuf[:, slot, kq * 1024:(kq + 1) * 1024],
                             in_=pall[:, 2 * b:2 * b + 2, :], func=AF.Exp,
                             scale=QSCALE,
                             accum_out=lparts[:, slot * 4 + kq:
                                              slot * 4 + kq + 1]),
                         wait=ew + [("c_v", lp_free.get(slot, 0))],
                         chain=False)
                    BK.set([2 * b, 2 * b + 1], "c_s", P.val("c_s"))
                P.op("vector", lambda v, slot=slot: v.tensor_reduce(
                    out=lsum[:, slot:slot + 1],
                    in_=lparts[:, slot * 4:slot * 4 + 4],
                    axis=AX.X, op=ALU.add),
                    wait=[("c_s", P.val("c_s"))], chain=False)
                lp_free[slot] = P.val("c_v")
                P.op("vector", lambda v, slot=slot: v.reciprocal(
                    out=lr[:, slot:slot + 1], in_=lsum[:, slot:slot + 1]))
                P.op("vector", lambda v, slot=slot: v.tensor_scalar(
                    out=rbc[:, slot, :], in0=ones_bf[:],
                    scalar1=lr[:, slot:slot + 1],
                    scalar2=None, op0=ALU.mult),
                    wait=[("c_t", rbc_free.get(slot, 0))])
                emit_colsum_piece(2)
            return slots

        for item in items:
            slots = emit_scores(item)
            assert state["pending"] is None
            state["pending"] = (item, slots, 0, P.val("c_v"))
        while state["pending"] is not None:
            emit_colsum_piece(2)
        P.flush_block()


def _phase3_outproj(nc, P, pall, wo_d, val_d, attn_d, x_d, r1_d, h2_d, eps_t):
    with nc.sbuf_tensor("wo_sb", [128, NCH, D], BF16) as wo_sb, \
         nc.sbuf_tensor("vslab", [128, 2, NCH, 512], BF16) as vslab, \
         nc.sbuf_tensor("attn_sb", [128, 1, NCH, 512], BF16) as attn_sb, \
         nc.sbuf_tensor("attnT", [128, 2, 4, D], BF16) as attnT, \
         nc.sbuf_tensor("p3xt2", [128, 2, D], F32) as xt2, \
         nc.sbuf_tensor("p3y1sb", [128, D], F32) as y1sb, \
         nc.sbuf_tensor("p3r1sb", [128, 2, D], F32) as r1sb, \
         nc.sbuf_tensor("r1ld", [128, D], F32) as r1ld, \
         nc.sbuf_tensor("p3h2sb", [128, 2, D], BF16) as h2sb, \
         nc.sbuf_tensor("p3bn2", [128, 1, 4, 6], F32) as bn2, \
         nc.sbuf_tensor("p3mvb1", [128, 2, 4, 2], F32) as mvb1, \
         nc.sbuf_tensor("p3mvb2", [128, 2, 4, 2], F32) as mvb2, \
         nc.sbuf_tensor("p3sqb1", [128, 4], F32) as sqb1, \
         nc.sbuf_tensor("p3rstd1", [128, 4], F32) as rstd1, \
         nc.sbuf_tensor("p3nmb1", [128, 4], F32) as nmb1, \
         nc.sbuf_tensor("p3sqb2", [128, 4], F32) as sqb2, \
         nc.sbuf_tensor("p3rstd2", [128, 4], F32) as rstd2, \
         nc.sbuf_tensor("p3nmb2", [128, 4], F32) as nmb2:

        for vf in range(NCH):
            P.dma("gpsimd", lambda g, vf=vf: g.dma_start(
                out=wo_sb[:, vf, :], in_=wo_d[vf * 128:(vf + 1) * 128, :]),
                "d_wa", batch=(vf > 0))
        wo_tgt = P.val("d_wa")

        BK = Banks()
        vload = {}
        attn_store = {}
        h2_done = {}
        stats1 = {}
        stats2 = {}
        mid_v = {}
        xt2_free = {0: 0, 1: 0}
        r1st_free = {0: 0, 1: 0}
        r1ld_free = {0: 0, 1: 0}
        h2st_free = {0: 0, 1: 0}
        r1st_done = {}

        def fold_front(tc):
            st = tc % 2
            psem = "d_sa" if tc % 2 == 0 else "d_sb"
            for j in range(4):
                tt = tc * 4 + j
                w = ([(psem, attn_store[tc]), ("c_v", mid_v.get(tc - 2, 0))]
                     if j == 0 else [])
                P.dma("scalar", lambda s, tt=tt, st=st, j=j: s.dma_start(
                    out=attnT[:, st, j, :],
                    in_=attn_d[:, tt * 128:(tt + 1) * 128], transpose=True),
                    "d_tr", wait=w, batch=(j > 0))
            tr = P.val("d_tr")
            P.wait("vector", [("d_tr", tr)])
            for j in range(4):
                for q in range(4):
                    P.op("vector", lambda v, st=st, j=j, q=q: v.bn_stats(
                        out=bn2[:, 0, q, :],
                        in_=attnT[:, st, j, q * 512:(q + 1) * 512]),
                        chain=(q == 0))
                P.op("vector", lambda v, st=st, j=j: v.bn_aggr(
                    out=mvb1[:, st, j, :], in_=bn2[:, 0, :, :]))
            stats1[tc] = P.val("c_v")

        def fold_mid(tc):
            st = tc % 2
            P.op("scalar", lambda s, st=st: s.activation(
                out=sqb1[:], in_=mvb1[:, st, :, 1], func=AF.Sqrt,
                bias=eps_t[:], scale=1.0), wait=[("c_v", stats1[tc])])
            sq = P.val("c_s")
            P.op("vector", lambda v: v.reciprocal(out=rstd1[:], in_=sqb1[:]),
                 wait=[("c_s", sq)])
            P.op("vector", lambda v, st=st: v.scalar_tensor_tensor(
                out=nmb1[:], in0=mvb1[:, st, :, 0], scalar=-1.0, in1=rstd1[:],
                op0=ALU.mult, op1=ALU.mult))
            for j in range(4):
                tt = tc * 4 + j
                jp = j % 2
                P.dma("gpsimd", lambda g, tt=tt, jp=jp: g.dma_start(
                    out=xt2[:, jp, :], in_=x_d[tt * 128:(tt + 1) * 128, :]),
                    "d_xa", wait=[("c_v", xt2_free[jp])])
                x_t = P.val("d_xa")
                P.op("vector", lambda v, jp=jp, j=j: v.tensor_scalar(
                    out=y1sb[:], in0=xt2[:, jp, :], scalar1=nmb1[:, j:j + 1],
                    scalar2=None, op0=ALU.add), wait=[("d_xa", x_t)])
                xt2_free[jp] = P.val("c_v")
                P.op("vector", lambda v, st=st, j=j, jp=jp:
                     v.scalar_tensor_tensor(
                         out=r1sb[:, jp, :], in0=attnT[:, st, j, :],
                         scalar=rstd1[:, j:j + 1], in1=y1sb[:],
                         op0=ALU.mult, op1=ALU.add),
                     wait=[("d_wv", r1st_free[jp])])
                P.dma("sync", lambda sy, tt=tt, jp=jp: sy.dma_start(
                    out=r1_d[tt * 128:(tt + 1) * 128, :], in_=r1sb[:, jp, :]),
                    "d_wv", wait=[("c_v", P.val("c_v"))])
                r1st_free[jp] = P.val("d_wv")
                for q in range(4):
                    P.op("vector", lambda v, jp=jp, q=q: v.bn_stats(
                        out=bn2[:, 0, q, :],
                        in_=r1sb[:, jp, q * 512:(q + 1) * 512]),
                        chain=(q == 0))
                P.op("vector", lambda v, st=st, j=j: v.bn_aggr(
                    out=mvb2[:, st, j, :], in_=bn2[:, 0, :, :]))
            mid_v[tc] = P.val("c_v")
            stats2[tc] = P.val("c_v")
            r1st_done[tc] = P.val("d_wv")

        def fold_tail(tc):
            st = tc % 2
            P.op("scalar", lambda s, st=st: s.activation(
                out=sqb2[:], in_=mvb2[:, st, :, 1], func=AF.Sqrt,
                bias=eps_t[:], scale=1.0), wait=[("c_v", stats2[tc])])
            sq = P.val("c_s")
            P.op("vector", lambda v: v.reciprocal(out=rstd2[:], in_=sqb2[:]),
                 wait=[("c_s", sq)])
            P.op("vector", lambda v, st=st: v.scalar_tensor_tensor(
                out=nmb2[:], in0=mvb2[:, st, :, 0], scalar=-1.0, in1=rstd2[:],
                op0=ALU.mult, op1=ALU.mult))
            for j in range(4):
                tt = tc * 4 + j
                jp = j % 2
                w = [("c_v", r1ld_free[0])]
                if j == 0:
                    w.append(("d_wv", r1st_done[tc]))
                P.dma("gpsimd", lambda g, tt=tt: g.dma_start(
                    out=r1ld[:], in_=r1_d[tt * 128:(tt + 1) * 128, :]),
                    "d_xb", wait=w)
                rt = P.val("d_xb")
                P.op("vector", lambda v, jp=jp, j=j: v.tensor_scalar(
                    out=h2sb[:, jp, :], in0=r1ld[:],
                    scalar1=rstd2[:, j:j + 1], scalar2=nmb2[:, j:j + 1],
                    op0=ALU.mult, op1=ALU.add),
                    wait=[("d_xb", rt), ("d_sg", h2st_free[jp])])
                r1ld_free[0] = P.val("c_v")
                P.dma("sync", lambda sy, tt=tt, jp=jp: sy.dma_start(
                    out=h2_d[tt * 128:(tt + 1) * 128, :], in_=h2sb[:, jp, :]),
                    "d_sg", wait=[("c_v", P.val("c_v"))])
                h2st_free[jp] = P.val("d_sg")

        def load_vslab(tc):
            P.dma("sync", lambda sy, tc=tc: sy.dma_start(
                out=vslab[:, tc % 2, 0:8, :],
                in_=val_d[0:1024, tc * 512:(tc + 1) * 512]
                .rearrange("(v p) t -> p v t", p=128)),
                "d_sv", wait=[("c_t", vload.get(tc - 2, 0))], batch=(tc > 0))
            P.dma("gpsimd", lambda sc, tc=tc: sc.dma_start(
                out=vslab[:, tc % 2, 8:16, :],
                in_=val_d[1024:2048, tc * 512:(tc + 1) * 512]
                .rearrange("(v p) t -> p v t", p=128)),
                "d_sv", batch=True, wait=[("c_t", vload.get(tc - 2, 0))])
            return P.val("d_sv")

        vs_tgt = {0: load_vslab(0)}
        for tc in range(NTC):
            vl_tgt = vs_tgt[tc]
            ssem2 = "d_sa" if tc % 2 == 0 else "d_sb"
            for fo in range(NCH):
                bank = fo % 4
                tw = [("d_sv", vl_tgt)] + BK.wait_for([bank])
                if tc == 0 and fo == 0:
                    tw.append(("d_wa", wo_tgt))
                P.wait("tensor", tw)
                for vf in range(NCH):
                    P.op("tensor", lambda t, vf=vf, fo=fo, tc=tc, bank=bank:
                         t.matmul(
                             pall[:, bank, :],
                             wo_sb[:, vf, fo * 128:(fo + 1) * 128],
                             vslab[:, tc % 2, vf, :],
                             start=(vf == 0), stop=(vf == NCH - 1),
                             skip_group_check=True),
                         inc=[("c_t", 1)] if vf == NCH - 1 else ())
                tpo = P.val("c_t")
                if fo == NCH - 1:
                    vload[tc] = tpo
                P.op("scalar", lambda s, fo=fo, bank=bank: s.activation(
                    out=attn_sb[:, 0, fo, :], in_=pall[:, bank, :],
                    func=AF.Copy),
                    wait=[("c_t", tpo),
                          (("d_sb" if tc % 2 == 0 else "d_sa"),
                           attn_store.get(tc - 1, 0))],
                    chain=False)
                BK.set([bank], "c_s", P.val("c_s"))
            attn_evac = P.val("c_s")
            P.dma("sync", lambda sy, tc=tc: sy.dma_start(
                out=attn_d[0:1024, tc * 512:(tc + 1) * 512]
                .rearrange("(f p) t -> p f t", p=128),
                in_=attn_sb[:, 0, 0:8, :]),
                ssem2, wait=[("c_s", attn_evac)])
            P.dma("gpsimd", lambda sc, tc=tc: sc.dma_start(
                out=attn_d[1024:2048, tc * 512:(tc + 1) * 512]
                .rearrange("(f p) t -> p f t", p=128),
                in_=attn_sb[:, 0, 8:16, :]),
                ssem2, batch=True, wait=[("c_s", attn_evac)])
            attn_store[tc] = P.val(ssem2)
            if tc + 1 < NTC:
                vs_tgt[tc + 1] = load_vslab(tc + 1)
            if tc >= 1:
                fold_mid(tc - 1)
            if tc >= 2:
                fold_tail(tc - 2)
                h2_done[tc - 2] = P.val("d_sg")
            fold_front(tc)
        fold_mid(NTC - 1)
        fold_tail(NTC - 2)
        h2_done[NTC - 2] = P.val("d_sg")
        fold_tail(NTC - 1)
        h2_done[NTC - 1] = P.val("d_sg")
        P.flush_block()
        return h2_done


def _phase4_ffn(nc, P, pall, sgt, sut, sdt, wg_d, wu_d, wd_d, h2_d, down_d,
                r1_d, out_d, eps_t, h2_done):
    e_ct = P.val("c_t")
    e_cs = P.val("c_s")
    e_cv = P.val("c_v")
    e_dsa = P.val("d_sa")
    e_dsb = P.val("d_sb")
    e_dwv = P.val("d_wv")
    with nc.sbuf_tensor("h2T", [128, 2, NCH, 512], BF16) as h2T, \
         nc.sbuf_tensor("wg_sb", [128, 3, NCH, 256], BF16) as wg_sb, \
         nc.sbuf_tensor("wu_sb", [128, 3, NCH, 256], BF16) as wu_sb, \
         nc.sbuf_tensor("wd_sb", [128, 2, 2, D], BF16) as wd_sb, \
         nc.sbuf_tensor("fuse", [128, NI, 512], BF16) as fuse, \
         nc.sbuf_tensor("sgl", [128, 2, 512], BF16) as sgl, \
         nc.sbuf_tensor("down_sb", [128, 2, 2, 512], BF16) as down_sb, \
         nc.sbuf_tensor("downT", [128, 4, D], BF16) as downT, \
         nc.sbuf_tensor("r1t", [128, D], F32) as r1t, \
         nc.sbuf_tensor("y2sb", [128, D], F32) as y2sb, \
         nc.sbuf_tensor("outsb", [128, D], F32) as outsb, \
         nc.sbuf_tensor("bn3", [128, 1, 4, 6], F32) as bn3, \
         nc.sbuf_tensor("mvb3", [128, 4, 2], F32) as mvb3, \
         nc.sbuf_tensor("sqb3", [128, 4], F32) as sqb3, \
         nc.sbuf_tensor("rstdb3", [128, 4], F32) as rstdb3, \
         nc.sbuf_tensor("nmbb3", [128, 4], F32) as nmbb3:

        NHG = 32
        BK = Banks()
        BK.set(list(range(8)), "c_s", e_cs)
        gu_done = {}
        wgu_free = {0: e_ct, 1: e_ct, 2: e_ct}
        wd_free = {0: e_ct, 1: e_ct}
        sgl_free = {0: e_cv, 1: e_cv}
        fuse_done = {}
        down_read = {}
        down_store_tc = {}
        down_store2 = {}

        preloaded = {}
        gctr = {"n": 0}
        epi = {"apply_v": e_cv, "stats_v": 0, "st": 0, "tr": 0, "first": True}

        def load_wgu(hg, q="gpsimd"):
            gb = gctr["n"] % 3
            gctr["n"] += 1
            P.dma(q, lambda gg, hg=hg, gb=gb: gg.dma_start(
                out=wg_sb[:, gb, :, :], in_=wg_d[:, hg, :, :]),
                "d_wa", wait=[("c_t", wgu_free[gb])], batch=(q == "scalar"))
            P.dma(q, lambda gg, hg=hg, gb=gb: gg.dma_start(
                out=wu_sb[:, gb, :, :], in_=wu_d[:, hg, :, :]),
                "d_wa", batch=True)
            return "d_wa", P.val("d_wa"), gb

        def emit_epilogue_front(tc):
            for j in range(4):
                tt = tc * 4 + j
                w = ([("d_sg", down_store_tc[tc]),
                      ("c_v", epi["apply_v"])] if j == 0 else [])
                P.dma("scalar", lambda s, tt=tt, j=j: s.dma_start(
                    out=downT[:, j, :],
                    in_=down_d[:, tt * 128:(tt + 1) * 128],
                    transpose=True), "d_xa", wait=w, batch=(j > 0))
            epi["tr"] = P.val("d_xa")
            P.wait("vector", [("d_xa", epi["tr"])])
            for j in range(4):
                for q in range(4):
                    P.op("vector", lambda v, j=j, q=q: v.bn_stats(
                        out=bn3[:, 0, q, :],
                        in_=downT[:, j, q * 512:(q + 1) * 512]),
                        chain=(j == 0 and q == 0))
                P.op("vector", lambda v, j=j: v.bn_aggr(
                    out=mvb3[:, j, :], in_=bn3[:, 0, :, :]))
            epi["stats_v"] = P.val("c_v")

        def emit_epilogue_back(tc):
            P.op("scalar", lambda s: s.activation(
                out=sqb3[:], in_=mvb3[:, :, 1], func=AF.Sqrt, bias=eps_t[:],
                scale=1.0), wait=[("c_v", epi["stats_v"])])
            sq_s = P.val("c_s")
            P.op("vector", lambda v: v.reciprocal(out=rstdb3[:], in_=sqb3[:]),
                 wait=[("c_s", sq_s)])
            P.op("vector", lambda v: v.scalar_tensor_tensor(
                out=nmbb3[:], in0=mvb3[:, :, 0], scalar=-1.0, in1=rstdb3[:],
                op0=ALU.mult, op1=ALU.mult))
            for j in range(4):
                tt = tc * 4 + j
                rw = [("c_v", epi["apply_v"])]
                if epi["first"]:
                    rw.append(("d_wv", e_dwv))
                    epi["first"] = False
                P.dma("sync", lambda sy, tt=tt: sy.dma_start(
                    out=r1t[:], in_=r1_d[tt * 128:(tt + 1) * 128, :]),
                    "d_xb", wait=rw)
                r1_tgt = P.val("d_xb")
                P.op("vector", lambda v, j=j: v.tensor_scalar(
                    out=y2sb[:], in0=r1t[:], scalar1=nmbb3[:, j:j + 1],
                    scalar2=None, op0=ALU.add),
                    wait=[("d_xb", r1_tgt), ("d_sa", epi["st"])])
                P.op("vector", lambda v, j=j: v.scalar_tensor_tensor(
                    out=outsb[:], in0=downT[:, j, :],
                    scalar=rstdb3[:, j:j + 1], in1=y2sb[:],
                    op0=ALU.mult, op1=ALU.add))
                epi["apply_v"] = P.val("c_v")
                P.dma("sync", lambda sy, tt=tt: sy.dma_start(
                    out=out_d[tt * 128:(tt + 1) * 128, :], in_=outsb[:]),
                    "d_sa", wait=[("c_v", epi["apply_v"])])
                epi["st"] = P.val("d_sa")

        def emit_h2T(tc):
            tb = tc % 2
            q = "scalar" if tc == 0 else "sync"
            for c in range(NCH):
                tr_w = ([("d_tr", P.val("d_tr")),
                         ("c_t", gu_done.get(tc - 2, e_ct)),
                         ("d_sg", h2_done[tc])] if c == 0 else [])
                P.dma(q, lambda s, tc=tc, tb=tb, c=c: s.dma_start(
                    out=h2T[:, tb, c, :],
                    in_=h2_d[tc * 512:(tc + 1) * 512, c * 128:(c + 1) * 128],
                    transpose=True),
                    "d_tr", wait=tr_w, batch=(c > 0))
            return P.val("d_tr")

        h2T_tgts = {}
        for tc in range(NTC):
            tb = tc % 2
            if tc not in h2T_tgts:
                h2T_tgts[tc] = emit_h2T(tc)
            h2T_tgt = h2T_tgts[tc]
            if tc >= 1:
                emit_epilogue_front(tc - 1)

            for hg in range(NHG):
                for la in (hg, hg + 1, hg + 2):
                    if la < NHG and (tc, la) not in preloaded:
                        preloaded[(tc, la)] = load_wgu(la)
                gsem, wg_tgt, gb = preloaded.pop((tc, hg))
                for i2 in range(2):
                    i = hg * 2 + i2
                    pg = (0, 1, 4)[i % 3]
                    pu = (2, 3, 5)[i % 3]
                    P.wait("tensor", [(gsem, wg_tgt), ("d_tr", h2T_tgt)]
                           + BK.wait_for([pg, pu]))
                    for c in range(NCH):
                        P.op("tensor", lambda t, c=c, i2=i2, gb=gb, tb=tb,
                             pg=pg: t.matmul(
                                 pall[:, pg, :],
                                 wg_sb[:, gb, c, i2 * 128:(i2 + 1) * 128],
                                 h2T[:, tb, c, :],
                                 start=(c == 0), stop=(c == NCH - 1),
                                 skip_group_check=True),
                             inc=[("c_t", 1)] if c == NCH - 1 else ())
                    tg = P.val("c_t")
                    for c in range(NCH):
                        P.op("tensor", lambda t, c=c, i2=i2, gb=gb, tb=tb,
                             pu=pu: t.matmul(
                                 pall[:, pu, :],
                                 wu_sb[:, gb, c, i2 * 128:(i2 + 1) * 128],
                                 h2T[:, tb, c, :],
                                 start=(c == 0), stop=(c == NCH - 1),
                                 skip_group_check=True),
                             inc=[("c_t", 1)] if c == NCH - 1 else ())
                    tu = P.val("c_t")
                    P.op("scalar", lambda s, i=i, pg=pg: s.activation(
                        out=sgl[:, i % 2, :], in_=pall[:, pg, :],
                        func=AF.Silu, scale=sgt[:, i:i + 1]),
                        wait=[("c_t", tg), ("c_v", sgl_free[i % 2])],
                        chain=False)
                    BK.set([pg], "c_s", P.val("c_s"))
                    fw = [("c_t", tu), ("c_s", P.val("c_s")),
                          ("c_t", down_read.get(tc - 1, 0))]
                    if tc == 0 and i == 0:
                        fw += [("d_sa", e_dsa), ("d_sb", e_dsb),
                               ("c_v", e_cv)]
                    P.op("vector", lambda v, i=i, pu=pu: v.scalar_tensor_tensor(
                        out=fuse[:, i, :], in0=pall[:, pu, :],
                        scalar=sut[:, i:i + 1], in1=sgl[:, i % 2, :],
                        op0=ALU.mult, op1=ALU.mult),
                        wait=fw, chain=False)
                    BK.set([pu], "c_v", P.val("c_v"))
                    sgl_free[i % 2] = P.val("c_v")
                wgu_free[gb] = P.val("c_t")
            gu_done[tc] = P.val("c_t")
            fuse_done[tc] = P.val("c_v")
            if tc >= 1:
                emit_epilogue_back(tc - 1)
            if tc + 1 < NTC:
                for hg in range(2):
                    preloaded[(tc + 1, hg)] = load_wgu(hg)

            for p in range(2):
                if p == 1 and tc + 1 < NTC:
                    h2T_tgts[tc + 1] = emit_h2T(tc + 1)
                for s2 in range(32):
                    wdb = s2 % 2
                    dsem = "d_wb" if wdb == 0 else "d_wv"
                    wdw = [("c_t", wd_free[wdb])]
                    if tc == 0 and p == 0 and s2 < 2:
                        wdw += [("d_sa", e_dsa), ("d_sb", e_dsb)]
                    P.dma("gpsimd", lambda gg, s2=s2, wdb=wdb: gg.dma_start(
                        out=wd_sb[:, wdb, :, :], in_=wd_d[:, s2, :, :]),
                        dsem, wait=wdw)
                    wd_tgt = P.val(dsem)
                    tw = [(dsem, wd_tgt), ("c_v", fuse_done[tc])]
                    if s2 == 0:
                        tw += BK.wait_for(list(range(8)))
                        tw.append(("c_s", P.val("c_s")))
                    P.wait("tensor", tw)
                    for q in range(2):
                        i = s2 * 2 + q
                        for f8 in range(8):
                            fo = p * 8 + f8
                            P.op("tensor", lambda t, q=q, f8=f8, fo=fo, i=i,
                                 wdb=wdb: t.matmul(
                                     pall[:, f8, :],
                                     wd_sb[:, wdb, q, fo * 128:(fo + 1) * 128],
                                     fuse[:, i, :],
                                     start=(i == 0), stop=(i == NI - 1),
                                     skip_group_check=True),
                                 inc=[("c_t", 1)]
                                 if (q == 1 and f8 == 7) else ())
                    wd_free[wdb] = P.val("c_t")
                tdn = P.val("c_t")
                for grp in range(4):
                    hb = grp % 2
                    for f2 in range(2):
                        f8 = grp * 2 + f2
                        fo = p * 8 + f8
                        dw = [("c_t", tdn),
                              ("d_sg", down_store2.get(hb, 0))]
                        if tc == 0 and p == 0 and grp < 2:
                            dw.append(("d_wv", e_dwv))
                        P.op("scalar", lambda s, f2=f2, f8=f8, fo=fo,
                             hb=hb: s.activation(
                                 out=down_sb[:, hb, f2, :],
                                 in_=pall[:, f8, :],
                                 func=AF.Copy, scale=sdt[:, fo:fo + 1]),
                             wait=dw, chain=False)
                    BK.set([grp * 2, grp * 2 + 1], "c_s", P.val("c_s"))
                    P.dma("sync", lambda sy, tc=tc, p=p, grp=grp, hb=hb:
                          sy.dma_start(
                              out=down_d[(p * 8 + grp * 2) * 128:
                                         (p * 8 + grp * 2 + 2) * 128,
                                         tc * 512:(tc + 1) * 512]
                              .rearrange("(f q) t -> q f t", q=128),
                              in_=down_sb[:, hb, :, :]),
                          "d_sg", wait=[("c_s", P.val("c_s"))])
                    down_store2[hb] = P.val("d_sg")
            down_store_tc[tc] = P.val("d_sg")
            down_read[tc] = P.val("c_t")
        emit_epilogue_front(NTC - 1)
        emit_epilogue_back(NTC - 1)
        P.flush_block()


def kernel(L0_attn_x, L0_attn_w_qkv, L0_attn_w_o, L0_ffn_w_gate, L0_ffn_w_up,
           L0_ffn_w_down, L0_ffn_s_gate, L0_ffn_s_up, L0_ffn_s_down):
    nc = build()
    x = np.ascontiguousarray(np.asarray(L0_attn_x, dtype=np.float32))
    shared = {
        "wqkv": np.ascontiguousarray(np.asarray(L0_attn_w_qkv, dtype=np.float32)),
        "wo": np.ascontiguousarray(np.asarray(L0_attn_w_o, dtype=np.float32)),
        "wg": np.ascontiguousarray(np.asarray(L0_ffn_w_gate, dtype=np.int8)
                                   .reshape(16, 128, 32, 256)
                                   .transpose(1, 2, 0, 3)),
        "wu": np.ascontiguousarray(np.asarray(L0_ffn_w_up, dtype=np.int8)
                                   .reshape(16, 128, 32, 256)
                                   .transpose(1, 2, 0, 3)),
        "wd": np.ascontiguousarray(np.asarray(L0_ffn_w_down, dtype=np.int8)
                                   .reshape(32, 2, 128, 2048)
                                   .transpose(2, 0, 1, 3)),
        "sg": np.ascontiguousarray(np.asarray(L0_ffn_s_gate, dtype=np.float32).reshape(NI, 128).T),
        "su": np.ascontiguousarray(np.asarray(L0_ffn_s_up, dtype=np.float32).reshape(NI, 128).T),
        "sd": np.ascontiguousarray(np.asarray(L0_ffn_s_down, dtype=np.float32).reshape(NCH, 128).T),
    }
    in_maps = [dict(shared, x=np.ascontiguousarray(x[b])) for b in range(8)]
    res = run_bass_kernel_spmd(nc, in_maps, core_ids=list(range(8)))
    return np.stack([res.results[b]["out"] for b in range(8)], axis=0)
